# version 01
# baseline (speedup 1.0000x reference)
import numpy as np
from contextlib import ExitStack

import concourse.bass as bass
import concourse.mybir as mybir
from concourse.bass_utils import run_bass_kernel_spmd

F32 = mybir.dt.float32
BF16 = mybir.dt.bfloat16
AF = mybir.ActivationFunctionType
ALU = mybir.AluOpType
AX = mybir.AxisListType

NCORES = 8
D = 1024
SEQ = 16384
TC = SEQ // NCORES
W = 512
NT = TC // W
NH = 16
DH = 64
DS = 48
DTR = 64


class _Res:
    __slots__ = ("name", "writer", "readers")

    def __init__(self, name):
        self.name = name
        self.writer = None
        self.readers = []


class Fw:
    ENG = ("sync", "scalar", "tensor", "vector", "gpsimd")

    def __init__(self, nc):
        self.nc = nc
        self.ops = {e: [] for e in self.ENG}
        self.cnt = {e: 0 for e in self.ENG}
        self._res = {}

    def res(self, name):
        r = self._res.get(name)
        if r is None:
            r = self._res[name] = _Res(name)
        return r

    def tok(self, eng):
        return (eng, self.cnt[eng])

    def op(self, eng, emit, reads=(), writes=(), dma=False, light=False, after=(), group=None):
        deps = {}

        def need(t):
            if t is None:
                return
            e, v = t
            if light and e == eng:
                return
            if deps.get(e, 0) < v:
                deps[e] = v

        rs = [self.res(x) if isinstance(x, str) else x for x in reads]
        ws = [self.res(x) if isinstance(x, str) else x for x in writes]
        for r in rs:
            need(r.writer)
        for w in ws:
            need(w.writer)
            for t in w.readers:
                need(t)
        for t in after:
            need(t)
        if dma:
            if group is None:
                group = ws[0].name if ws else rs[0].name
            key = "dma:" + group
            inc = 16
        else:
            key = eng
            inc = 1
        myv = self.cnt.get(key, 0) + inc
        self.ops[eng].append((emit, deps, inc, key))
        self.cnt[key] = myv
        t = (key, myv)
        for w in ws:
            w.writer = t
            w.readers = []
        for r in rs:
            r.readers.append(t)
        return t

    def settle(self, group, names):
        key = "dma:" + group
        t = (key, self.cnt[key])
        for n in names:
            self.res(n).writer = t

    def build(self, blk, sems):
        final = dict(self.cnt)

        for e in self.ENG:
            ops = self.ops[e]
            if not ops and e != "sync":
                continue

            def driver(eng, e=e, ops=ops):
                waited = {}
                for emit, deps, inc, key in ops:
                    for de, dv in deps.items():
                        if waited.get(de, 0) < dv:
                            eng.wait_ge(sems[de], dv)
                            waited[de] = dv
                    inst = emit(eng)
                    inst.then_inc(sems[key], inc)
                if e == "sync":
                    for k, v in final.items():
                        if v > 0 and waited.get(k, 0) < v:
                            eng.wait_ge(sems[k], v)

            getattr(blk, e)(driver)


class Prog:
    def __init__(self):
        self.nc = bass.Bass("TRN2", target_bir_lowering=False)
        self.fw = Fw(self.nc)
        self.stack = ExitStack()

    def dram(self, name, shape, dtype=F32, kind="ExternalInput"):
        return self.nc.dram_tensor(name, list(shape), dtype, kind=kind).ap()

    def sb(self, name, shape, dtype=F32):
        return self.stack.enter_context(self.nc.sbuf_tensor(name, list(shape), dtype))

    def ps(self, name, shape, dtype=F32):
        return self.stack.enter_context(self.nc.psum_tensor(name, list(shape), dtype))

    def finish(self):
        nc, fw = self.nc, self.fw
        sems = {}
        for k in fw.cnt:
            sems[k] = self.stack.enter_context(nc.semaphore(f"sem_{k}".replace(":", "_")))
        blk = self.stack.enter_context(nc.Block())
        fw.build(blk, sems)
        self.stack.close()
        return nc


def dma(p, q, out, in_, reads=(), writes=(), group=None, **kw):
    return p.fw.op(q, lambda e: e.dma_start(out=out, in_=in_, **kw), reads=reads, writes=writes,
                   dma=True, group=group)


def mm(p, out, lhsT, rhs, start, stop, reads=(), writes=()):
    return p.fw.op("tensor", lambda e: e.matmul(out, lhsT, rhs, start=start, stop=stop),
                   reads=reads, writes=writes, light=True)


def act(p, out, in_, func, bias=0.0, scale=1.0, accum_out=None, reads=(), writes=()):
    def f(e):
        if accum_out is not None:
            return e.activation(out, in_, func, bias=bias, scale=scale, accum_out=accum_out)
        return e.activation(out, in_, func, bias=bias, scale=scale)
    return p.fw.op("scalar", f, reads=reads, writes=writes)


def vop(p, emit, reads=(), writes=(), eng="vector"):
    return p.fw.op(eng, emit, reads=reads, writes=writes)


def build_s1():
    p = Prog()
    WH = W + 3

    xT = p.dram("xT", [D, TC + 3])
    pw_wT = p.dram("pw_wT", [D, D])
    prm = p.dram("prm", [128, 8 * 4 + 8 * 3 + 1])
    xpj_bd = p.dram("xpj_bd", [128, 128])
    dtpj_bd = p.dram("dtpj_bd", [128, 128])
    cst = p.dram("cst", [128, 8])

    o_xln = p.dram("xln", [D, TC], kind="ExternalOutput")
    o_u = p.dram("u", [D, TC], kind="ExternalOutput")
    o_dt = p.dram("dt", [D, TC], kind="ExternalOutput")

    s_pwT = [p.sb(f"pwT{k}", [128, D], BF16) for k in range(8)]
    s_xpj = p.sb("xpj", [128, 128], BF16)
    s_dtpj = p.sb("dtpj", [128, 128], BF16)
    s_prm = p.sb("c_prm", [128, 57])
    s_ones1 = p.sb("ones1", [1, 128])
    s_onesb = p.sb("onesb", [128, 1], BF16)
    s_cst = p.sb("c_cst", [128, 8])

    for k in range(8):
        dma(p, "gpsimd", s_pwT[k][:], pw_wT[k * 128:(k + 1) * 128, :], writes=[f"pwT{k}"])
    dma(p, "gpsimd", s_xpj[:], xpj_bd, writes=["xpj"])
    dma(p, "gpsimd", s_dtpj[:], dtpj_bd, writes=["dtpj"])
    dma(p, "sync", s_prm[:], prm, writes=["prm"])
    dma(p, "sync", s_cst[:], cst, writes=["cst"])
    vop(p, lambda e: e.memset(s_ones1[:], 1.0), writes=["ones1"])
    vop(p, lambda e: e.memset(s_onesb[:], 1.0), writes=["onesb"])
    DWK = 0
    DWB = 32
    PWB = 40
    N1W = 48
    DTB = 56

    s_x = [p.sb(f"x{i}", [128, WH]) for i in range(8)]
    s_xbf = [p.sb(f"xbf{i}", [128, WH], BF16) for i in range(8)]
    s_xsq = [p.sb(f"xsq{i}", [128, WH], BF16) for i in range(8)]
    s_t1 = [p.sb(f"t1_{i}", [128, WH]) for i in range(8)]
    s_xln = [p.sb(f"xln{i}", [128, W]) for i in range(8)]
    s_u = [p.sb(f"u{i}", [128, WH]) for i in range(8)]
    s_ucbf = [p.sb(f"ucbf{i}", [128, W], BF16) for i in range(8)]
    s_xh = [p.sb(f"xh{i}", [128, W]) for i in range(8)]
    s_xhbf = [p.sb(f"xhbf{i}", [128, W], BF16) for i in range(8)]
    s_dtxbf = p.sb("dtxbf", [128, W], BF16)
    s_sv = p.sb("sv", [128, 8 * W])
    s_sp1 = p.sb("sp1", [128, 8 * W])
    s_sp2 = p.sb("sp2", [128, 8 * W])

    s_mu = p.sb("mu", [1, WH])
    s_ex2 = p.sb("ex2", [1, WH])
    s_var = p.sb("var", [1, WH])
    s_rstd = p.sb("rstd", [1, WH])
    s_ra = p.sb("ra", [1, WH])
    s_rowc = p.sb("rowc", [1, WH])
    s_mub = p.sb("mub", [128, WH])
    s_rstdb = p.sb("rstdb", [128, WH])
    s_rowcb = p.sb("rowcb", [128, WH])

    P_stx = p.ps("Pstx", [128, 1024])
    P_stq = p.ps("Pstq", [128, 1024])
    P_pw = [p.ps("Ppw0", [128, W]), p.ps("Ppw1", [128, W])]
    P_xp = p.ps("Pxp", [128, W])

    def bcast(dst_ps, psname, row_ap, row_name, ncols):
        c = 0
        while c < ncols:
            n = min(512, ncols - c)
            mm(p, dst_ps[0:128, c:c + n], s_ones1[:], row_ap[0:1, c:c + n], True, True,
               reads=["ones1", row_name], writes=[psname])
            c += n

    for tt in range(NT):
        c0 = tt * W
        for i in range(8):
            dma(p, "sync", s_x[i][:], xT[i * 128:(i + 1) * 128, c0:c0 + WH], writes=[f"x{i}"])
        for i in range(8):
            act(p, s_xsq[i][:], s_x[i][:], AF.Square, reads=[f"x{i}"], writes=[f"xsq{i}"])
            vop(p, lambda e, i=i: e.tensor_copy(s_xbf[i][:], s_x[i][:]),
                reads=[f"x{i}"], writes=[f"xbf{i}"])
        for c, n in ((0, 512), (512, 3)):
            for i in range(8):
                mm(p, P_stx[0:1, c:c + n], s_onesb[:], s_xbf[i][:, c:c + n], i == 0, i == 7,
                   reads=["onesb", f"xbf{i}"], writes=["Pstx"])
            for i in range(8):
                mm(p, P_stq[0:1, c:c + n], s_onesb[:], s_xsq[i][:, c:c + n], i == 0, i == 7,
                   reads=["onesb", f"xsq{i}"], writes=["Pstq"])
        vop(p, lambda e: e.tensor_scalar_mul(s_mu[:], P_stx[0:1, 0:WH], 1.0 / D), reads=["Pstx"], writes=["mu"])
        vop(p, lambda e: e.tensor_scalar_mul(s_ex2[:], P_stq[0:1, 0:WH], 1.0 / D), reads=["Pstq"], writes=["ex2"])
        vop(p, lambda e: e.tensor_mul(s_var[:], s_mu[:], s_mu[:]), reads=["mu"], writes=["var"])
        vop(p, lambda e: e.tensor_sub(s_var[:], s_ex2[:], s_var[:]), reads=["ex2", "var"], writes=["var"])
        act(p, s_rstd[:], s_var[:], AF.Sqrt, bias=s_cst[0:1, 0:1], reads=["var", "cst"], writes=["rstd"])
        vop(p, lambda e: e.reciprocal(s_rstd[:], s_rstd[:]), reads=["rstd"], writes=["rstd"])
        vop(p, lambda e: e.tensor_mul(s_ra[:], s_var[:], s_rstd[:]), reads=["var", "rstd"], writes=["ra"])
        vop(p, lambda e: e.tensor_mul(s_ra[:], s_ra[:], s_rstd[:]), reads=["ra", "rstd"], writes=["ra"])
        act(p, s_ra[:], s_ra[:], AF.Sqrt, bias=s_cst[0:1, 1:2], reads=["ra", "cst"], writes=["ra"])
        vop(p, lambda e: e.reciprocal(s_ra[:], s_ra[:]), reads=["ra"], writes=["ra"])
        vop(p, lambda e: e.tensor_mul(s_rowc[:], s_ra[:], s_rstd[:]), reads=["ra", "rstd"], writes=["rowc"])
        bcast(P_stx, "Pstx", s_mu, "mu", WH)
        act(p, s_mub[:], P_stx[0:128, 0:WH], AF.Copy, reads=["Pstx"], writes=["mub"])
        bcast(P_stq, "Pstq", s_rstd, "rstd", WH)
        act(p, s_rstdb[:], P_stq[0:128, 0:WH], AF.Copy, reads=["Pstq"], writes=["rstdb"])
        bcast(P_stx, "Pstx", s_rowc, "rowc", WH)
        act(p, s_rowcb[:], P_stx[0:128, 0:WH], AF.Copy, reads=["Pstx"], writes=["rowcb"])

        for i in range(8):
            vop(p, lambda e, i=i: e.tensor_sub(s_t1[i][:], s_x[i][:], s_mub[:]),
                reads=[f"x{i}", "mub"], writes=[f"t1{i}"])
            vop(p, lambda e, i=i: e.tensor_mul(s_xln[i][:], s_t1[i][:, 3:WH], s_rstdb[:, 3:WH]),
                reads=[f"t1{i}", "rstdb"], writes=[f"xln{i}"])
            vop(p, lambda e, i=i: e.scalar_tensor_tensor(s_u[i][:], s_t1[i][:], s_prm[:, N1W + i:N1W + i + 1], s_rowcb[:], ALU.mult, ALU.mult),
                reads=[f"t1{i}", "prm", "rowcb"], writes=[f"u{i}"])
            vop(p, lambda e, i=i: e.tensor_scalar(s_xh[i][:], s_u[i][:, 0:W], s_prm[:, DWK + 4 * i:DWK + 4 * i + 1], s_prm[:, DWB + i:DWB + i + 1], ALU.mult, ALU.add),
                reads=[f"u{i}", "prm"], writes=[f"xh{i}"])
            for j in (1, 2):
                vop(p, lambda e, i=i, j=j: e.scalar_tensor_tensor(s_xh[i][:], s_u[i][:, j:j + W], s_prm[:, DWK + 4 * i + j:DWK + 4 * i + j + 1], s_xh[i][:], ALU.mult, ALU.add),
                    reads=[f"u{i}", "prm", f"xh{i}"], writes=[f"xh{i}"])
            vop(p, lambda e, i=i: e.scalar_tensor_tensor(s_ucbf[i][:], s_u[i][:, 3:3 + W], s_prm[:, DWK + 4 * i + 3:DWK + 4 * i + 4], s_xh[i][:], ALU.mult, ALU.add),
                reads=[f"u{i}", "prm", f"xh{i}"], writes=[f"ucbf{i}"])
        for m in range(8):
            pp = P_pw[m % 2]
            ppn = f"Ppw{m % 2}"
            for k in range(8):
                mm(p, pp[:], s_pwT[k][:, m * 128:(m + 1) * 128], s_ucbf[k][:], k == 0, k == 7,
                   reads=[f"pwT{k}", f"ucbf{k}"], writes=[ppn])
            vop(p, lambda e, m=m, pp=pp: e.tensor_scalar(s_xh[m][:], pp[:], s_prm[:, PWB + m:PWB + m + 1], None, ALU.add),
                reads=[ppn, "prm"], writes=[f"xh{m}"])
            act(p, s_xhbf[m][:], s_xh[m][:], AF.Copy, reads=[f"xh{m}"], writes=[f"xhbf{m}"])
            dma(p, "gpsimd", o_u[m * 128:(m + 1) * 128, tt * W:(tt + 1) * W], s_xh[m][:], reads=[f"xh{m}"])
            dma(p, "gpsimd", o_xln[m * 128:(m + 1) * 128, tt * W:(tt + 1) * W], s_xln[m][:], reads=[f"xln{m}"])
        for hp in range(8):
            mm(p, P_xp[:], s_xpj[:], s_xhbf[hp][:], True, True,
               reads=["xpj", f"xhbf{hp}"], writes=["Pxp"])
            act(p, s_dtxbf[:], P_xp[:], AF.Copy, reads=["Pxp"], writes=["dtxbf"])
            mm(p, P_xp[:], s_dtpj[:], s_dtxbf[:], True, True,
               reads=["dtpj", "dtxbf"], writes=["Pxp"])
            vop(p, lambda e, hp=hp: e.tensor_scalar(s_sv[:, hp * W:(hp + 1) * W], P_xp[:], s_prm[:, DTB:DTB + 1], None, ALU.add),
                reads=["Pxp", "prm"], writes=["sv"])
        act(p, s_sp1[:], s_sv[:], AF.Abs, reads=["sv"], writes=["sp1"])
        act(p, s_sp1[:], s_sp1[:], AF.Exp, scale=s_cst[:, 3:4], reads=["sp1", "cst"], writes=["sp1"])
        act(p, s_sp1[:], s_sp1[:], AF.Ln, bias=s_cst[:, 2:3], reads=["sp1", "cst"], writes=["sp1"])
        vop(p, lambda e: e.tensor_scalar_max(s_sp2[:], s_sv[:], 0.0), reads=["sv"], writes=["sp2"])
        vop(p, lambda e: e.tensor_add(s_sp2[:], s_sp2[:], s_sp1[:]), reads=["sp1", "sp2"], writes=["sp2"])
        for hp in range(8):
            dma(p, "gpsimd", o_dt[hp * 128:(hp + 1) * 128, tt * W:(tt + 1) * W], s_sp2[:, hp * W:(hp + 1) * W],
                reads=["sp2"])
    return p


def _prep_s1_maps(inputs):
    xT = np.ascontiguousarray(inputs["L0_attn_x"][0].T).astype(np.float32)
    xTh = np.pad(xT, ((0, 0), (3, 0)))
    pw_wT = np.ascontiguousarray(inputs["L0_attn_pw_w"].T).astype(np.float32)
    xpw = np.asarray(inputs["L0_attn_x_proj_w"], np.float32)
    dtw = np.asarray(inputs["L0_attn_dt_proj_w"], np.float32)
    xpj = np.zeros((128, 128), np.float32)
    xpj[0:64, 0:64] = xpw[0:64].T
    xpj[64:128, 64:128] = xpw[0:64].T
    dtpj = np.zeros((128, 128), np.float32)
    dtpj[0:64, 0:64] = dtw.T
    dtpj[64:128, 64:128] = dtw.T
    prm = np.zeros((128, 57), np.float32)
    dwk = np.asarray(inputs["L0_attn_dw_kernel"], np.float32)[:, 0, :].reshape(8, 128, 4)
    for m in range(8):
        for j in range(4):
            prm[:, m * 4 + j] = dwk[m, :, j]
    prm[:, 32:40] = np.asarray(inputs["L0_attn_dw_bias"], np.float32).reshape(8, 128).T
    prm[:, 40:48] = np.asarray(inputs["L0_attn_pw_b"], np.float32).reshape(8, 128).T
    prm[:, 48:56] = np.asarray(inputs["L0_attn_norm1_w"], np.float32).reshape(8, 128).T
    prm[:, 56] = np.tile(np.asarray(inputs["L0_attn_dt_proj_b"], np.float32), 2)
    cstv = np.zeros((128, 8), np.float32)
    cstv[:, 0] = 1e-5
    cstv[:, 1] = 1e-6
    cstv[:, 2] = 1.0
    cstv[:, 3] = -1.0
    maps = []
    for c in range(NCORES):
        maps.append({
            "xT": np.ascontiguousarray(xTh[:, c * TC:c * TC + TC + 3]),
            "pw_wT": pw_wT,
            "prm": prm,
            "xpj_bd": xpj,
            "dtpj_bd": dtpj,
            "cst": cstv,
        })
    return maps


def run_s1(inputs):
    p = build_s1()
    nc = p.finish()
    maps = _prep_s1_maps(inputs)
    res = run_bass_kernel_spmd(nc, maps, core_ids=list(range(NCORES)))
    xln = np.concatenate([r["xln"] for r in res.results], axis=1)
    u = np.concatenate([r["u"] for r in res.results], axis=1)
    dt = np.concatenate([r["dt"] for r in res.results], axis=1)
    return xln, u, dt


def kernel(**inputs):
    raise NotImplementedError


S2T = 2048

def build_s2():
    p = Prog()
    NCH = SEQ // S2T

    dt = p.dram("dt", [128, SEQ])
    u = p.dram("u", [128, SEQ])
    wrep = p.dram("wrep", [96, 128, 128])
    alog = p.dram("alog", [128, DS])
    o_y = p.dram("y", [128, SEQ], kind="ExternalOutput")

    s_wrep = [p.sb(f"wr{j}", [128, 128], BF16) for j in range(96)]
    for j in range(96):
        dma(p, "gpsimd", s_wrep[j][:], wrep[j, :, :], writes=[f"wr{j}"], group="wrepall")
    p.fw.settle("wrepall", [f"wr{j}" for j in range(96)])
    s_alog = p.sb("c_alog", [128, DS])
    s_nega = p.sb("nega", [128, DS])
    dma(p, "sync", s_alog[:], alog, writes=["alog"])
    act(p, s_nega[:], s_alog[:], AF.Exp, reads=["alog"], writes=["nega"])
    vop(p, lambda e: e.tensor_scalar_mul(s_nega[:], s_nega[:], -1.0), reads=["nega"], writes=["nega"])
    s_state = p.sb("state", [128, DS])
    vop(p, lambda e: e.memset(s_state[:], 0.0), writes=[f"st{s}" for s in range(DS)])

    s_dt = [p.sb(f"dtc{j}", [128, S2T]) for j in range(2)]
    s_u = [p.sb(f"uc{j}", [128, S2T]) for j in range(2)]
    s_ubf = [p.sb(f"ubf{j}", [128, S2T], BF16) for j in range(2)]
    s_dtu = [p.sb(f"dtu{j}", [128, S2T]) for j in range(2)]
    s_a = [p.sb(f"ab{j}", [128, S2T]) for j in range(2)]
    s_xin = [p.sb(f"xin{j}", [128, S2T]) for j in range(2)]
    s_H = [p.sb(f"H{j}", [128, S2T]) for j in range(2)]
    s_hc = p.sb("hc", [128, S2T])
    s_y = p.sb("yb", [128, S2T])
    P_B = p.ps("PB", [128, S2T])
    P_C = p.ps("PC", [128, S2T])

    for c in range(NCH):
        par = c % 2
        t0 = c * S2T
        dma(p, "sync", s_dt[par][:], dt[:, t0:t0 + S2T], writes=[f"dtc{par}"])
        dma(p, "sync", s_u[par][:], u[:, t0:t0 + S2T], writes=[f"uc{par}"])
        dma(p, "gpsimd", s_ubf[par][:], u[:, t0:t0 + S2T], writes=[f"ubf{par}"])
        vop(p, lambda e, par=par: e.tensor_mul(s_dtu[par][:], s_dt[par][:], s_u[par][:]),
            reads=[f"dtc{par}", f"uc{par}"], writes=[f"dtu{par}"])
        for s in range(DS):
            sp = s % 2
            for j in range(4):
                mm(p, P_B[:, j * 512:(j + 1) * 512], s_wrep[2 * s][:], s_ubf[par][:, j * 512:(j + 1) * 512],
                   True, True, reads=[f"wr{2 * s}", f"ubf{par}"], writes=["PB"])
                mm(p, P_C[:, j * 512:(j + 1) * 512], s_wrep[2 * s + 1][:], s_ubf[par][:, j * 512:(j + 1) * 512],
                   True, True, reads=[f"wr{2 * s + 1}", f"ubf{par}"], writes=["PC"])
            act(p, s_a[sp][:], s_dt[par][:], AF.Exp, scale=s_nega[:, s:s + 1],
                reads=[f"dtc{par}", "nega"], writes=[f"ab{sp}"])
            vop(p, lambda e, sp=sp, par=par: e.tensor_mul(s_xin[sp][:], s_dtu[par][:], P_B[:]),
                reads=[f"dtu{par}", "PB"], writes=[f"xin{sp}"])
            vop(p, lambda e, sp=sp, s=s: e.tensor_tensor_scan(s_H[sp][:], s_a[sp][:], s_xin[sp][:],
                                                              s_state[:, s:s + 1], ALU.mult, ALU.add),
                reads=[f"ab{sp}", f"xin{sp}", f"st{s}"], writes=[f"H{sp}"])
            vop(p, lambda e, sp=sp, s=s: e.tensor_copy(s_state[:, s:s + 1], s_H[sp][:, S2T - 1:S2T]),
                reads=[f"H{sp}"], writes=[f"st{s}"])
            if s == 0:
                vop(p, lambda e, sp=sp: e.tensor_mul(s_y[:], s_H[sp][:], P_C[:]),
                    reads=[f"H{sp}", "PC"], writes=["yb"])
            else:
                vop(p, lambda e, sp=sp: e.tensor_mul(s_hc[:], s_H[sp][:], P_C[:]),
                    reads=[f"H{sp}", "PC"], writes=["hc"])
                vop(p, lambda e: e.tensor_add(s_y[:], s_y[:], s_hc[:]),
                    reads=["hc", "yb"], writes=["yb"])
        dma(p, "gpsimd", o_y[:, t0:t0 + S2T], s_y[:], reads=["yb"])
    return p


def _prep_s2_maps(inputs, dt_full, u_full):
    xpw = np.asarray(inputs["L0_attn_x_proj_w"], np.float32)
    alog = np.asarray(inputs["L0_attn_A_log"], np.float32)
    wrep = np.zeros((96, 128, 128), np.float32)
    for s in range(DS):
        rb = np.repeat(xpw[DTR + s][:, None], 64, axis=1)
        rc = np.repeat(xpw[DTR + DS + s][:, None], 64, axis=1)
        wrep[2 * s, 0:64, 0:64] = rb
        wrep[2 * s, 64:128, 64:128] = rb
        wrep[2 * s + 1, 0:64, 0:64] = rc
        wrep[2 * s + 1, 64:128, 64:128] = rc
    maps = []
    for c in range(NCORES):
        al = np.zeros((128, DS), np.float32)
        al[0:64] = alog[2 * c]
        al[64:128] = alog[2 * c + 1]
        maps.append({
            "dt": np.ascontiguousarray(dt_full[c * 128:(c + 1) * 128]),
            "u": np.ascontiguousarray(u_full[c * 128:(c + 1) * 128]),
            "wrep": wrep,
            "alog": al,
        })
    return maps


def run_s2(inputs, dt_full, u_full):
    p = build_s2()
    nc = p.finish()
    maps = _prep_s2_maps(inputs, dt_full, u_full)
    res = run_bass_kernel_spmd(nc, maps, core_ids=list(range(NCORES)))
    return np.concatenate([r["y"] for r in res.results], axis=0)


def build_s3():
    p = Prog()
    xT = p.dram("xT", [D, TC])
    xln = p.dram("xln", [D, TC])
    y = p.dram("y", [D, TC])
    opT = p.dram("opT", [D, D])
    gT = p.dram("gT", [D, 2 * D])
    uT = p.dram("uT", [D, 2 * D])
    dT = p.dram("dT", [2 * D, D])
    prm = p.dram("prm", [128, 16])
    cst = p.dram("cst", [128, 8])
    o_x1 = p.dram("x1", [D, TC], kind="ExternalOutput")

    s_opT = [p.sb(f"opT{k}", [128, D], BF16) for k in range(8)]
    s_gT = [p.sb(f"gT{k}", [128, 2 * D], BF16) for k in range(8)]
    s_uT = [p.sb(f"uT{k}", [128, 2 * D], BF16) for k in range(8)]
    s_dT = [p.sb(f"dT{k}", [128, D], BF16) for k in range(16)]
    for k in range(8):
        dma(p, "gpsimd", s_opT[k][:], opT[k * 128:(k + 1) * 128, :], writes=[f"opT{k}"], group="wload")
        dma(p, "gpsimd", s_gT[k][:], gT[k * 128:(k + 1) * 128, :], writes=[f"gT{k}"], group="wload")
        dma(p, "gpsimd", s_uT[k][:], uT[k * 128:(k + 1) * 128, :], writes=[f"uT{k}"], group="wload")
    for k in range(16):
        dma(p, "gpsimd", s_dT[k][:], dT[k * 128:(k + 1) * 128, :], writes=[f"dT{k}"], group="wload")
    p.fw.settle("wload", [f"opT{k}" for k in range(8)] + [f"gT{k}" for k in range(8)]
                + [f"uT{k}" for k in range(8)] + [f"dT{k}" for k in range(16)])
    s_prm = p.sb("c_prm", [128, 16])
    s_cst = p.sb("c_cst", [128, 8])
    s_ones1 = p.sb("ones1", [1, 128])
    s_onesb = p.sb("onesb", [128, 1], BF16)
    dma(p, "sync", s_prm[:], prm, writes=["prm"])
    dma(p, "sync", s_cst[:], cst, writes=["cst"])
    vop(p, lambda e: e.memset(s_ones1[:], 1.0), writes=["ones1"])
    vop(p, lambda e: e.memset(s_onesb[:], 1.0), writes=["onesb"])

    s_ybf = [p.sb(f"ybf{i}", [128, W], BF16) for i in range(8)]
    s_xln = [p.sb(f"xl{i}", [128, W]) for i in range(8)]
    s_x = [p.sb(f"rx{i}", [128, W]) for i in range(8)]
    s_x1 = [p.sb(f"sx1_{i}", [128, W]) for i in range(8)]
    s_t = p.sb("tshared", [128, W])
    s_h2 = [p.sb(f"h2_{i}", [128, W], BF16) for i in range(8)]
    s_sact = p.sb("sact", [128, W], BF16)
    s_gu = [p.sb(f"gu{o}", [128, W], BF16) for o in range(16)]
    s_row = p.sb("row", [1, W])

    P_a = [p.ps("Pa0", [128, W]), p.ps("Pa1", [128, W])]
    P_b = [p.ps("Pb0", [128, W]), p.ps("Pb1", [128, W])]
    P_st = p.ps("Pst", [128, W])
    P_bc = p.ps("Pbc", [128, W])

    for tt in range(NT):
        c0 = tt * W
        for i in range(8):
            dma(p, "gpsimd", s_ybf[i][:], y[i * 128:(i + 1) * 128, c0:c0 + W], writes=[f"ybf{i}"])
            dma(p, "sync", s_xln[i][:], xln[i * 128:(i + 1) * 128, c0:c0 + W], writes=[f"xl{i}"])
            dma(p, "sync", s_x[i][:], xT[i * 128:(i + 1) * 128, c0:c0 + W], writes=[f"rx{i}"])
        for m in range(8):
            pp, ppn = P_a[m % 2], f"Pa{m % 2}"
            for k in range(8):
                mm(p, pp[:], s_opT[k][:, m * 128:(m + 1) * 128], s_ybf[k][:], k == 0, k == 7,
                   reads=[f"opT{k}", f"ybf{k}"], writes=[ppn])
            vop(p, lambda e, m=m, pp=pp: e.scalar_tensor_tensor(s_x1[m][:], pp[:], s_prm[:, m:m + 1], s_xln[m][:], ALU.add, ALU.add),
                reads=[ppn, "prm", f"xl{m}"], writes=[f"x1_{m}"])
            vop(p, lambda e, m=m: e.tensor_add(s_x[m][:], s_x1[m][:], s_x[m][:]),
                reads=[f"x1_{m}", f"rx{m}"], writes=[f"rx{m}"])
        for m in range(8):
            act(p, s_ybf[m][:], s_x1[m][:], AF.Square, reads=[f"x1_{m}"], writes=[f"ybf{m}"])
        for i in range(8):
            mm(p, P_st[0:1, 0:W], s_onesb[:], s_ybf[i][:], i == 0, i == 7,
               reads=["onesb", f"ybf{i}"], writes=["Pst"])
        vop(p, lambda e: e.tensor_scalar_mul(s_row[:], P_st[0:1, 0:W], 1.0 / D), reads=["Pst"], writes=["row"])
        act(p, s_row[:], s_row[:], AF.Sqrt, bias=s_cst[0:1, 1:2], reads=["row", "cst"], writes=["row"])
        vop(p, lambda e: e.reciprocal(s_row[:], s_row[:]), reads=["row"], writes=["row"])
        mm(p, P_bc[0:128, 0:W], s_ones1[:], s_row[0:1, 0:W], True, True,
           reads=["ones1", "row"], writes=["Pbc"])
        for m in range(8):
            vop(p, lambda e, m=m: e.scalar_tensor_tensor(s_h2[m][:], s_x1[m][:], s_prm[:, 8 + m:9 + m], P_bc[:], ALU.mult, ALU.mult),
                reads=[f"x1_{m}", "prm", "Pbc"], writes=[f"h2_{m}"])
        for mo in range(16):
            pg, pgn = P_a[mo % 2], f"Pa{mo % 2}"
            pu, pun = P_b[mo % 2], f"Pb{mo % 2}"
            for k in range(8):
                mm(p, pg[:], s_gT[k][:, mo * 128:(mo + 1) * 128], s_h2[k][:], k == 0, k == 7,
                   reads=[f"gT{k}", f"h2_{k}"], writes=[pgn])
            for k in range(8):
                mm(p, pu[:], s_uT[k][:, mo * 128:(mo + 1) * 128], s_h2[k][:], k == 0, k == 7,
                   reads=[f"uT{k}", f"h2_{k}"], writes=[pun])
            act(p, s_sact[:], pg[:], AF.Silu, reads=[pgn], writes=["sact"])
            vop(p, lambda e, mo=mo, pu=pu: e.tensor_mul(s_gu[mo][:], s_sact[:], pu[:]),
                reads=["sact", pun], writes=[f"gu{mo}"])
        for m in range(8):
            pp, ppn = P_a[m % 2], f"Pa{m % 2}"
            for ko in range(16):
                mm(p, pp[:], s_dT[ko][:, m * 128:(m + 1) * 128], s_gu[ko][:], ko == 0, ko == 15,
                   reads=[f"dT{ko}", f"gu{ko}"], writes=[ppn])
            vop(p, lambda e, m=m, pp=pp: e.scalar_tensor_tensor(s_x1[m][:], pp[:], 1.0, s_x[m][:], ALU.mult, ALU.add),
                reads=[ppn, f"rx{m}", f"x1_{m}"], writes=[f"x1_{m}"]),
            dma(p, "gpsimd", o_x1[m * 128:(m + 1) * 128, c0:c0 + W], s_x1[m][:], reads=[f"x1_{m}"])
    return p


def _prep_s3_maps(inputs, xT_full, xln_full, y_full):
    opT = np.ascontiguousarray(np.asarray(inputs["L0_attn_out_proj_w"], np.float32).T)
    gT = np.ascontiguousarray(np.asarray(inputs["L0_attn_gate_w"], np.float32).T)
    uT = np.ascontiguousarray(np.asarray(inputs["L0_attn_up_w"], np.float32).T)
    dT = np.ascontiguousarray(np.asarray(inputs["L0_attn_down_w"], np.float32).T)
    prm = np.zeros((128, 16), np.float32)
    prm[:, 0:8] = np.asarray(inputs["L0_attn_out_proj_b"], np.float32).reshape(8, 128).T
    prm[:, 8:16] = np.asarray(inputs["L0_attn_norm2_w"], np.float32).reshape(8, 128).T
    cstv = np.zeros((128, 8), np.float32)
    cstv[:, 0] = 1e-5
    cstv[:, 1] = 1e-6
    cstv[:, 2] = 1.0
    cstv[:, 3] = -1.0
    maps = []
    for c in range(NCORES):
        sl = slice(c * TC, (c + 1) * TC)
        maps.append({
            "xT": np.ascontiguousarray(xT_full[:, sl]),
            "xln": np.ascontiguousarray(xln_full[:, sl]),
            "y": np.ascontiguousarray(y_full[:, sl]),
            "opT": opT, "gT": gT, "uT": uT, "dT": dT,
            "prm": prm, "cst": cstv,
        })
    return maps


def run_s3(inputs, xT_full, xln_full, y_full):
    p = build_s3()
    nc = p.finish()
    maps = _prep_s3_maps(inputs, xT_full, xln_full, y_full)
    res = run_bass_kernel_spmd(nc, maps, core_ids=list(range(NCORES)))
    return np.concatenate([r["x1"] for r in res.results], axis=1)


def build_s4():
    p = Prog()
    xT = p.dram("xT", [D, TC])
    w1T = p.dram("w1T", [D, 4096])
    w1r = p.dram("w1r", [4096, D])
    w2T = p.dram("w2T", [4096, D])
    w2r = p.dram("w2r", [D, 4096])
    prm = p.dram("prm", [128, 48])
    cst = p.dram("cst", [128, 8])
    o_x = p.dram("xo", [D, TC], kind="ExternalOutput")

    s_w1 = [p.sb(f"sw1_{k}", [128, 4096], BF16) for k in range(8)]
    s_w2 = [p.sb(f"sw2_{k}", [128, D], BF16) for k in range(32)]
    s_a1 = p.sb("a1", [128, 32])
    s_a2 = p.sb("a2", [128, 8])
    s_scr = p.sb("scr", [128, 2048])
    s_a2t = p.sb("a2t", [128, 8])
    s_prm = p.sb("c_prm", [128, 48])
    s_cst = p.sb("c_cst", [128, 8])
    s_ones1 = p.sb("ones1", [1, 128])
    s_onesb = p.sb("onesb", [128, 1], BF16)
    dma(p, "sync", s_prm[:], prm, writes=["prm"])
    dma(p, "sync", s_cst[:], cst, writes=["cst"])
    vop(p, lambda e: e.memset(s_ones1[:], 1.0), writes=["ones1"])
    vop(p, lambda e: e.memset(s_onesb[:], 1.0), writes=["onesb"])

    for k in range(8):
        for hlf in range(2):
            dma(p, "sync", s_scr[:], w1T[k * 128:(k + 1) * 128, hlf * 2048:(hlf + 1) * 2048], writes=["scr"])
            act(p, s_w1[k][:, hlf * 2048:(hlf + 1) * 2048], s_scr[:], AF.Sign, reads=["scr"], writes=[f"sw1_{k}"])
    for k in range(32):
        dma(p, "sync", s_scr[:, 0:D], w1r[k * 128:(k + 1) * 128, :], writes=["scr"])
        vop(p, lambda e, k=k: e.tensor_reduce(s_a1[:, k:k + 1], s_scr[:, 0:D], AX.X, ALU.add, apply_absolute_value=True),
            reads=["scr"], writes=["a1"])
        dma(p, "sync", s_scr[:, 0:D], w2T[k * 128:(k + 1) * 128, :], writes=["scr"])
        act(p, s_w2[k][:], s_scr[:, 0:D], AF.Sign, reads=["scr"], writes=[f"sw2_{k}"])
    vop(p, lambda e: e.tensor_scalar_mul(s_a1[:], s_a1[:], 1.0 / D), reads=["a1"], writes=["a1"])
    for k in range(8):
        dma(p, "sync", s_scr[:], w2r[k * 128:(k + 1) * 128, 0:2048], writes=["scr"])
        vop(p, lambda e, k=k: e.tensor_reduce(s_a2[:, k:k + 1], s_scr[:], AX.X, ALU.add, apply_absolute_value=True),
            reads=["scr"], writes=["a2"])
        dma(p, "sync", s_scr[:], w2r[k * 128:(k + 1) * 128, 2048:4096], writes=["scr"])
        vop(p, lambda e, k=k: e.tensor_reduce(s_a2t[:, k:k + 1], s_scr[:], AX.X, ALU.add, apply_absolute_value=True),
            reads=["scr"], writes=["a2t"])
    vop(p, lambda e: e.tensor_add(s_a2[:], s_a2[:], s_a2t[:]), reads=["a2", "a2t"], writes=["a2"])
    vop(p, lambda e: e.tensor_scalar_mul(s_a2[:], s_a2[:], 1.0 / 4096.0), reads=["a2"], writes=["a2"])

    s_x = [p.sb(f"rx{i}", [128, W]) for i in range(8)]
    s_xbf = [p.sb(f"xbf{i}", [128, W], BF16) for i in range(8)]
    s_t1 = p.sb("t1s", [128, W])
    s_h = [p.sb(f"hb{o}", [128, W], BF16) for o in range(32)]
    s_mu = p.sb("mu", [1, W])
    s_var = p.sb("var", [1, W])
    s_rstd = p.sb("rstd", [1, W])
    s_out = p.sb("outs", [128, W])
    P_stx = p.ps("Pstx", [128, W])
    P_stq = p.ps("Pstq", [128, W])
    P_mu = p.ps("Pmu", [128, W])
    P_rs = p.ps("Prs", [128, W])
    P_a = [p.ps("Pa0", [128, W]), p.ps("Pa1", [128, W])]

    for tt in range(NT):
        c0 = tt * W
        for i in range(8):
            dma(p, "sync", s_x[i][:], xT[i * 128:(i + 1) * 128, c0:c0 + W], writes=[f"rx{i}"])
            act(p, s_h[i][:], s_x[i][:], AF.Square, reads=[f"rx{i}"], writes=[f"hb{i}"])
            vop(p, lambda e, i=i: e.tensor_copy(s_xbf[i][:], s_x[i][:]), reads=[f"rx{i}"], writes=[f"xbf{i}"])
        for i in range(8):
            mm(p, P_stx[0:1, 0:W], s_onesb[:], s_xbf[i][:], i == 0, i == 7,
               reads=["onesb", f"xbf{i}"], writes=["Pstx"])
        for i in range(8):
            mm(p, P_stq[0:1, 0:W], s_onesb[:], s_h[i][:], i == 0, i == 7,
               reads=["onesb", f"hb{i}"], writes=["Pstq"])
        vop(p, lambda e: e.tensor_scalar_mul(s_mu[:], P_stx[0:1, 0:W], 1.0 / D), reads=["Pstx"], writes=["mu"])
        vop(p, lambda e: e.tensor_scalar_mul(s_rstd[:], P_stq[0:1, 0:W], 1.0 / D), reads=["Pstq"], writes=["rstd"])
        vop(p, lambda e: e.tensor_mul(s_var[:], s_mu[:], s_mu[:]), reads=["mu"], writes=["var"])
        vop(p, lambda e: e.tensor_sub(s_var[:], s_rstd[:], s_var[:]), reads=["rstd", "var"], writes=["var"])
        act(p, s_rstd[:], s_var[:], AF.Sqrt, bias=s_cst[0:1, 0:1], reads=["var", "cst"], writes=["rstd"])
        vop(p, lambda e: e.reciprocal(s_rstd[:], s_rstd[:]), reads=["rstd"], writes=["rstd"])
        mm(p, P_mu[0:128, 0:W], s_ones1[:], s_mu[0:1, 0:W], True, True, reads=["ones1", "mu"], writes=["Pmu"])
        mm(p, P_rs[0:128, 0:W], s_ones1[:], s_rstd[0:1, 0:W], True, True, reads=["ones1", "rstd"], writes=["Prs"])
        for i in range(8):
            vop(p, lambda e, i=i: e.scalar_tensor_tensor(s_t1[:], P_mu[:], -1.0, s_x[i][:], ALU.mult, ALU.add),
                reads=["Pmu", f"rx{i}"], writes=["t1s"])
            vop(p, lambda e, i=i: e.tensor_mul(s_xbf[i][:], s_t1[:], P_rs[:]),
                reads=["t1s", "Prs"], writes=[f"xbf{i}"])
        for mo in range(32):
            pp, ppn = P_a[mo % 2], f"Pa{mo % 2}"
            for k in range(8):
                mm(p, pp[:], s_w1[k][:, mo * 128:(mo + 1) * 128], s_xbf[k][:], k == 0, k == 7,
                   reads=[f"sw1_{k}", f"xbf{k}"], writes=[ppn])
            act(p, s_h[mo][:], pp[:], AF.Gelu, bias=s_prm[:, mo:mo + 1], scale=s_a1[:, mo:mo + 1],
                reads=[ppn, "prm", "a1"], writes=[f"hb{mo}"])
        for m in range(8):
            pp, ppn = P_a[m % 2], f"Pa{m % 2}"
            for ko in range(32):
                mm(p, pp[:], s_w2[ko][:, m * 128:(m + 1) * 128], s_h[ko][:], ko == 0, ko == 31,
                   reads=[f"sw2_{ko}", f"hb{ko}"], writes=[ppn])
            vop(p, lambda e, m=m, pp=pp: e.tensor_scalar(s_out[:], pp[:], s_a2[:, m:m + 1], s_prm[:, 32 + m:33 + m], ALU.mult, ALU.add),
                reads=[ppn, "a2", "prm"], writes=["outs"])
            vop(p, lambda e, m=m: e.tensor_add(s_out[:], s_out[:], s_x[m][:]),
                reads=["outs", f"rx{m}"], writes=["outs"])
            dma(p, "gpsimd", o_x[m * 128:(m + 1) * 128, c0:c0 + W], s_out[:], reads=["outs"])
    return p


def _prep_s4_maps(inputs, xT_full):
    w1 = np.asarray(inputs["L0_ffn_w1"], np.float32)
    w2 = np.asarray(inputs["L0_ffn_w2"], np.float32)
    prm = np.zeros((128, 48), np.float32)
    prm[:, 0:32] = np.asarray(inputs["L0_ffn_b1"], np.float32).reshape(32, 128).T
    prm[:, 32:40] = np.asarray(inputs["L0_ffn_b2"], np.float32).reshape(8, 128).T
    cstv = np.zeros((128, 8), np.float32)
    cstv[:, 0] = 1e-5
    cstv[:, 1] = 1e-6
    w1T = np.ascontiguousarray(w1.T)
    w2T = np.ascontiguousarray(w2.T)
    maps = []
    for c in range(NCORES):
        maps.append({
            "xT": np.ascontiguousarray(xT_full[:, c * TC:(c + 1) * TC]),
            "w1T": w1T, "w1r": w1, "w2T": w2T, "w2r": w2,
            "prm": prm, "cst": cstv,
        })
    return maps


def run_s4(inputs, xT_full):
    p = build_s4()
    nc = p.finish()
    maps = _prep_s4_maps(inputs, xT_full)
    res = run_bass_kernel_spmd(nc, maps, core_ids=list(range(NCORES)))
    return np.concatenate([r["xo"] for r in res.results], axis=1)


def build_s5():
    p = Prog()
    xT = p.dram("xT", [D, TC])
    WvT = p.dram("WvT", [D, D])
    WqT = p.dram("WqT", [D, D])
    WkT = p.dram("WkT", [D, D])
    WoT = p.dram("WoT", [D, D])
    prm = p.dram("prm", [128, 32])
    beta = p.dram("beta", [1, 16])
    blk1 = p.dram("blk1", [128, 2])
    blk2 = p.dram("blk2", [2, 128])
    cst = p.dram("cst", [128, 8])
    scr = p.dram("scr16", [1, 16], kind="Internal")
    o_x = p.dram("xo", [D, TC], kind="ExternalOutput")

    s_Wv = [p.sb(f"Wv{k}", [128, D], BF16) for k in range(8)]
    s_Wq = [p.sb(f"Wq{k}", [128, D], BF16) for k in range(8)]
    s_Wk = [p.sb(f"Wk{k}", [128, D], BF16) for k in range(8)]
    s_Wo = [p.sb(f"Wo{k}", [128, D], BF16) for k in range(8)]
    for k in range(8):
        dma(p, "gpsimd", s_Wv[k][:], WvT[k * 128:(k + 1) * 128, :], writes=[f"Wv{k}"], group="wl")
        dma(p, "gpsimd", s_Wq[k][:], WqT[k * 128:(k + 1) * 128, :], writes=[f"Wq{k}"], group="wl")
        dma(p, "gpsimd", s_Wk[k][:], WkT[k * 128:(k + 1) * 128, :], writes=[f"Wk{k}"], group="wl")
        dma(p, "gpsimd", s_Wo[k][:], WoT[k * 128:(k + 1) * 128, :], writes=[f"Wo{k}"], group="wl")
    p.fw.settle("wl", [f"W{t}{k}" for t in "vqko" for k in range(8)])
    s_prm = p.sb("c_prm", [128, 32])
    s_cst = p.sb("c_cst", [128, 8])
    s_blk1 = p.sb("c_blk1", [128, 2], BF16)
    s_blk2 = p.sb("c_blk2", [2, 128])
    s_beta = p.sb("c_beta", [1, 16])
    s_scl = p.sb("scl", [2, 8])
    s_ones1 = p.sb("ones1", [1, 128])
    s_onesb = p.sb("onesb", [128, 1], BF16)
    dma(p, "sync", s_prm[:], prm, writes=["prm"])
    dma(p, "sync", s_cst[:], cst, writes=["cst"])
    dma(p, "gpsimd", s_blk1[:], blk1, writes=["blk1"])
    dma(p, "sync", s_blk2[:], blk2, writes=["blk2"])
    dma(p, "sync", s_beta[:], beta, writes=["beta"])
    vop(p, lambda e: e.memset(s_ones1[:], 1.0), writes=["ones1"])
    vop(p, lambda e: e.memset(s_onesb[:], 1.0), writes=["onesb"])
    act(p, s_beta[:], s_beta[:], AF.Exp, reads=["beta"], writes=["beta"])
    vop(p, lambda e: e.tensor_scalar_mul(s_beta[:], s_beta[:], 8.0), reads=["beta"], writes=["beta"])
    vop(p, lambda e: e.reciprocal(s_beta[:], s_beta[:]), reads=["beta"], writes=["beta"])
    dma(p, "sync", scr, s_beta[:], reads=["beta"], writes=["scrd"])
    dma(p, "sync", s_scl[:], scr.rearrange("a (c r) -> r (a c)", r=2), reads=["scrd"], writes=["scl"], allow_slow_non_contiguous=True)

    s_x = [p.sb(f"rx{i}", [128, W]) for i in range(8)]
    s_xbf = [p.sb(f"xbf{i}", [128, W], BF16) for i in range(8)]
    s_sq = [p.sb(f"sq{i}", [128, W], BF16) for i in range(8)]
    s_v = [p.sb(f"vv{i}", [128, W]) for i in range(8)]
    s_qs = p.sb("qs", [128, W])
    s_qk = p.sb("qk", [128, W], BF16)
    s_ctx = [p.sb(f"ctx{i}", [128, W], BF16) for i in range(8)]
    s_sc = p.sb("sc", [2, 8 * W])
    s_e = p.sb("se", [2, 8 * W])
    s_at = p.sb("sat", [2, 8 * W])
    s_t1 = p.sb("t1s", [128, W])
    s_mu = p.sb("mu", [1, W])
    s_var = p.sb("var", [1, W])
    s_rstd = p.sb("rstd", [1, W])
    s_out = p.sb("outs", [128, W])
    P_stx = p.ps("Pstx", [128, W])
    P_stq = p.ps("Pstq", [128, W])
    P_mu = p.ps("Pmu", [128, W])
    P_rs = p.ps("Prs", [128, W])
    P_v = p.ps("Pv", [128, W])
    P_q = p.ps("Pq", [128, W])
    P_k = p.ps("Pk", [128, W])
    P_sc = p.ps("Psc", [128, W])

    for tt in range(NT):
        c0 = tt * W
        for i in range(8):
            dma(p, "sync", s_x[i][:], xT[i * 128:(i + 1) * 128, c0:c0 + W], writes=[f"rx{i}"])
            act(p, s_sq[i][:], s_x[i][:], AF.Square, reads=[f"rx{i}"], writes=[f"sq{i}"])
            vop(p, lambda e, i=i: e.tensor_copy(s_xbf[i][:], s_x[i][:]), reads=[f"rx{i}"], writes=[f"xbf{i}"])
        for i in range(8):
            mm(p, P_stx[0:1, 0:W], s_onesb[:], s_xbf[i][:], i == 0, i == 7,
               reads=["onesb", f"xbf{i}"], writes=["Pstx"])
        for i in range(8):
            mm(p, P_stq[0:1, 0:W], s_onesb[:], s_sq[i][:], i == 0, i == 7,
               reads=["onesb", f"sq{i}"], writes=["Pstq"])
        vop(p, lambda e: e.tensor_scalar_mul(s_mu[:], P_stx[0:1, 0:W], 1.0 / D), reads=["Pstx"], writes=["mu"])
        vop(p, lambda e: e.tensor_scalar_mul(s_rstd[:], P_stq[0:1, 0:W], 1.0 / D), reads=["Pstq"], writes=["rstd"])
        vop(p, lambda e: e.tensor_mul(s_var[:], s_mu[:], s_mu[:]), reads=["mu"], writes=["var"])
        vop(p, lambda e: e.tensor_sub(s_var[:], s_rstd[:], s_var[:]), reads=["rstd", "var"], writes=["var"])
        act(p, s_rstd[:], s_var[:], AF.Sqrt, bias=s_cst[0:1, 0:1], reads=["var", "cst"], writes=["rstd"])
        vop(p, lambda e: e.reciprocal(s_rstd[:], s_rstd[:]), reads=["rstd"], writes=["rstd"])
        mm(p, P_mu[0:128, 0:W], s_ones1[:], s_mu[0:1, 0:W], True, True, reads=["ones1", "mu"], writes=["Pmu"])
        mm(p, P_rs[0:128, 0:W], s_ones1[:], s_rstd[0:1, 0:W], True, True, reads=["ones1", "rstd"], writes=["Prs"])
        for i in range(8):
            vop(p, lambda e, i=i: e.scalar_tensor_tensor(s_t1[:], P_mu[:], -1.0, s_x[i][:], ALU.mult, ALU.add),
                reads=["Pmu", f"rx{i}"], writes=["t1s"])
            vop(p, lambda e, i=i: e.tensor_mul(s_xbf[i][:], s_t1[:], P_rs[:]),
                reads=["t1s", "Prs"], writes=[f"xbf{i}"])
        for m in range(8):
            for k in range(8):
                mm(p, P_v[:], s_Wv[k][:, m * 128:(m + 1) * 128], s_xbf[k][:], k == 0, k == 7,
                   reads=[f"Wv{k}", f"xbf{k}"], writes=["Pv"])
            vop(p, lambda e, m=m: e.tensor_scalar(s_v[m][:], P_v[:], s_prm[:, m:m + 1], None, ALU.add),
                reads=["Pv", "prm"], writes=[f"vv{m}"])
            for k in range(8):
                mm(p, P_q[:], s_Wq[k][:, m * 128:(m + 1) * 128], s_xbf[k][:], k == 0, k == 7,
                   reads=[f"Wq{k}", f"xbf{k}"], writes=["Pq"])
            vop(p, lambda e, m=m: e.tensor_scalar(s_qs[:], P_q[:], s_prm[:, 8 + m:9 + m], None, ALU.add),
                reads=["Pq", "prm"], writes=["qs"])
            for k in range(8):
                mm(p, P_k[:], s_Wk[k][:, m * 128:(m + 1) * 128], s_xbf[k][:], k == 0, k == 7,
                   reads=[f"Wk{k}", f"xbf{k}"], writes=["Pk"])
            vop(p, lambda e, m=m: e.scalar_tensor_tensor(s_qk[:], P_k[:], s_prm[:, 16 + m:17 + m], s_qs[:], ALU.add, ALU.mult),
                reads=["Pk", "prm", "qs"], writes=["qk"])
            mm(p, P_sc[0:2, 0:W], s_blk1[:], s_qk[:], True, True, reads=["blk1", "qk"], writes=["Psc"])
            vop(p, lambda e, m=m: e.tensor_scalar(s_sc[0:2, m * W:(m + 1) * W], P_sc[0:2, 0:W], s_scl[0:2, m:m + 1], None, ALU.mult),
                reads=["Psc", "scl"], writes=["ssc"])
        vop(p, lambda e: e.tensor_sub(s_e[:], s_sc[:], s_sc[:]), reads=["ssc"], writes=["sse"])
        act(p, s_e[:], s_e[:], AF.Exp, reads=["sse"], writes=["sse"])
        vop(p, lambda e: e.reciprocal(s_at[:], s_e[:]), reads=["sse"], writes=["sat"])
        vop(p, lambda e: e.tensor_mul(s_at[:], s_at[:], s_e[:]), reads=["sat", "sse"], writes=["sat"])
        for m in range(8):
            mm(p, P_v[0:128, 0:W], s_blk2[:], s_at[0:2, m * W:(m + 1) * W], True, True,
               reads=["blk2", "sat"], writes=["Pv"])
            vop(p, lambda e, m=m: e.tensor_mul(s_ctx[m][:], s_v[m][:], P_v[:]),
                reads=[f"vv{m}", "Pv"], writes=[f"ctx{m}"])
        for m in range(8):
            for k in range(8):
                mm(p, P_q[:], s_Wo[k][:, m * 128:(m + 1) * 128], s_ctx[k][:], k == 0, k == 7,
                   reads=[f"Wo{k}", f"ctx{k}"], writes=["Pq"])
            vop(p, lambda e, m=m: e.scalar_tensor_tensor(s_out[:], P_q[:], s_prm[:, 24 + m:25 + m], s_x[m][:], ALU.add, ALU.add),
                reads=["Pq", "prm", f"rx{m}"], writes=["outs"])
            dma(p, "gpsimd", o_x[m * 128:(m + 1) * 128, c0:c0 + W], s_out[:], reads=["outs"])
    return p


def _prep_s5_maps(inputs, xT_full):
    prm = np.zeros((128, 32), np.float32)
    prm[:, 0:8] = np.asarray(inputs["L1_attn_bv"], np.float32).reshape(8, 128).T
    prm[:, 8:16] = np.asarray(inputs["L1_attn_bq"], np.float32).reshape(8, 128).T
    prm[:, 16:24] = np.asarray(inputs["L1_attn_bk"], np.float32).reshape(8, 128).T
    prm[:, 24:32] = np.asarray(inputs["L1_attn_bo"], np.float32).reshape(8, 128).T
    cstv = np.zeros((128, 8), np.float32)
    cstv[:, 0] = 1e-5
    cstv[:, 1] = 1e-6
    blk1 = np.zeros((128, 2), np.float32)
    blk1[0:64, 0] = 1.0
    blk1[64:128, 1] = 1.0
    blk2 = np.zeros((2, 128), np.float32)
    blk2[0, 0:64] = 1.0
    blk2[1, 64:128] = 1.0
    maps = []
    for c in range(NCORES):
        maps.append({
            "xT": np.ascontiguousarray(xT_full[:, c * TC:(c + 1) * TC]),
            "WvT": np.ascontiguousarray(np.asarray(inputs["L1_attn_Wv"], np.float32).T),
            "WqT": np.ascontiguousarray(np.asarray(inputs["L1_attn_Wq"], np.float32).T),
            "WkT": np.ascontiguousarray(np.asarray(inputs["L1_attn_Wk"], np.float32).T),
            "WoT": np.ascontiguousarray(np.asarray(inputs["L1_attn_Wo"], np.float32).T),
            "prm": prm, "beta": np.asarray(inputs["L1_attn_beta"], np.float32).reshape(1, 16),
            "blk1": blk1, "blk2": blk2, "cst": cstv,
        })
    return maps


def run_s5(inputs, xT_full):
    p = build_s5()
    nc = p.finish()
    maps = _prep_s5_maps(inputs, xT_full)
    res = run_bass_kernel_spmd(nc, maps, core_ids=list(range(NCORES)))
    return np.concatenate([r["xo"] for r in res.results], axis=1)


TH = 2560

def build_s6(with_ln):
    p = Prog()
    f_in = p.dram("f", [D, TH + 1])
    wqkv = p.dram("wqkv", [D, 1536])
    wo = p.dram("wo", [512, D])
    qsr = p.dram("qsr", [1, 64])
    ksr = p.dram("ksr", [1, 64])
    cq8 = p.dram("cq8", [TH, 64])
    snq = p.dram("snq", [TH, 64])
    ck = p.dram("ck", [TH, 64])
    snk = p.dram("snk", [TH, 64])
    amask = p.dram("amask", [4, 128, 4096], BF16)
    iden = p.dram("iden", [128, 128])
    cst = p.dram("cst", [128, 8])
    o_f = p.dram("fo", [D, TC], kind="ExternalOutput")
    if with_ln:
        o_z = p.dram("zo", [D, TC], kind="ExternalOutput")

    s_wq = [p.sb(f"wq{k}", [128, 1536], BF16) for k in range(8)]
    s_wo = [p.sb(f"swo{r}", [128, D], BF16) for r in range(4)]
    for k in range(8):
        dma(p, "gpsimd", s_wq[k][:], wqkv[k * 128:(k + 1) * 128, :], writes=[f"wq{k}"], group="wl")
    for r in range(4):
        dma(p, "gpsimd", s_wo[r][:], wo[r * 128:(r + 1) * 128, :], writes=[f"swo{r}"], group="wl")
    p.fw.settle("wl", [f"wq{k}" for k in range(8)] + [f"swo{r}" for r in range(4)])
    s_iden = p.sb("c_iden", [128, 128], BF16)
    dma(p, "gpsimd", s_iden[:], iden, writes=["iden"])
    s_cst = p.sb("c_cst", [128, 8])
    dma(p, "sync", s_cst[:], cst, writes=["cst"])
    s_ones1 = p.sb("ones1", [1, 128])
    s_onesb = p.sb("onesb", [128, 1], BF16)
    vop(p, lambda e: e.memset(s_ones1[:], 1.0), writes=["ones1"])
    vop(p, lambda e: e.memset(s_onesb[:], 1.0), writes=["onesb"])
    s_qsrow = p.sb("qsrow", [1, 64])
    s_ksrow = p.sb("ksrow", [1, 64])
    dma(p, "sync", s_qsrow[:], qsr, writes=["qsrow"])
    dma(p, "sync", s_ksrow[:], ksr, writes=["ksrow"])
    s_qsb = p.sb("qsb", [128, 64])
    s_ksb = p.sb("ksb", [128, 64])
    s_msk = p.sb("mskb", [128, 4 * 1024], BF16)

    s_x = [p.sb(f"rx{i}", [128, 513]) for i in range(8)]
    s_xbf = [p.sb(f"xbf{i}", [128, 513], BF16) for i in range(8)]
    if with_ln:
        s_sq = [p.sb(f"sq{i}", [128, 513], BF16) for i in range(8)]
        s_z = [p.sb(f"zz{i}", [128, 513]) for i in range(8)]
    s_qkv = [p.sb(f"qkv{mo}", [128, W], BF16) for mo in range(12)]
    s_vtm = [p.sb(f"vtm{jc}", [128, 512], BF16) for jc in range(20)]
    s_k2 = [p.sb(f"k2_{r}", [128, TH], BF16) for r in range(4)]
    s_q2 = [p.sb(f"q2_{r}", [128, TC], BF16) for r in range(4)]
    s_tm = p.sb("tm", [128, 512])
    s_ss = p.sb("ss", [128, 8])
    s_rr = p.sb("rr", [128, 8])
    s_t1 = p.sb("t1b", [128, 512])
    s_t2 = p.sb("t2b", [128, 512])
    s_tbl = p.sb("tbl", [128, 128])
    s_tmbf = p.sb("tmbf", [128, 512], BF16)
    s_scs = p.sb("scs", [128, 1024])
    s_nmx = p.sb("nmx", [128, 1])
    s_abf = p.sb("abf", [128, 1024], BF16)
    s_rs = p.sb("rsum", [128, 1])
    s_aT = [p.sb(f"aT{kc}", [128, 128], BF16) for kc in range(8)]
    s_otm = p.sb("otm", [128, 512])
    s_ofm = [p.sb(f"ofm{r}", [128, TC], BF16) for r in range(4)]
    s_fr = p.sb("fr", [128, W])
    s_mu = p.sb("mu", [1, 513])
    s_var = p.sb("var", [1, 513])
    s_rstd = p.sb("rstd", [1, 513])

    P_A = p.ps("PA", [128, 1024])
    P_B = p.ps("PB", [128, 1024])
    P_C = [p.ps("PC0", [128, W]), p.ps("PC1", [128, W])]
    P_T = p.ps("PT", [128, 128], BF16)
    P_X = p.ps("PX", [128, 64])

    mm(p, P_X[0:128, 0:64], s_ones1[:], s_qsrow[0:1, 0:64], True, True, reads=["ones1", "qsrow"], writes=["PX"])
    act(p, s_qsb[:], P_X[:], AF.Copy, reads=["PX"], writes=["qsb"])
    mm(p, P_X[0:128, 0:64], s_ones1[:], s_ksrow[0:1, 0:64], True, True, reads=["ones1", "ksrow"], writes=["PX"])
    act(p, s_ksb[:], P_X[:], AF.Copy, reads=["PX"], writes=["ksb"])

    def h3(buf):
        return buf[:].rearrange("p (h d) -> p h d", h=8)

    def tmchunk_rotary(jc, isq):
        ct, st = (cq8, snq) if isq else (ck, snk)
        scb, scn = (s_qsb, "qsb") if isq else (s_ksb, "ksb")
        dma(p, "sync", s_tbl[:, 0:64], ct[jc * 128:(jc + 1) * 128, :], writes=["tbl"], group="tbl")
        dma(p, "sync", s_tbl[:, 64:128], st[jc * 128:(jc + 1) * 128, :], writes=["tbl"], group="tbl")
        vop(p, lambda e: e.tensor_mul(s_t2[:], s_tm[:], s_tm[:]), reads=["tm"], writes=["t2b"])
        vop(p, lambda e: e.tensor_reduce(s_ss[:].unsqueeze(2), h3(s_t2), AX.X, ALU.add),
            reads=["t2b"], writes=["ss"])
        act(p, s_rr[:], s_ss[:], AF.Sqrt, reads=["ss"], writes=["rr"])
        vop(p, lambda e: e.tensor_scalar_max(s_rr[:], s_rr[:], 1e-12), reads=["rr"], writes=["rr"])
        vop(p, lambda e: e.reciprocal(s_rr[:], s_rr[:]), reads=["rr"], writes=["rr"])
        vop(p, lambda e: e.tensor_tensor(h3(s_t1), h3(s_tm),
                                         s_rr[:].unsqueeze(2).broadcast_to([128, 8, 64]), ALU.mult),
            reads=["tm", "rr"], writes=["t1b"])
        vop(p, lambda e: e.tensor_tensor(h3(s_t1), h3(s_t1),
                                         scb[:].unsqueeze(1).broadcast_to([128, 8, 64]), ALU.mult),
            reads=["t1b", scn], writes=["t1b"])
        vop(p, lambda e: e.tensor_tensor(h3(s_t2)[:, :, 0:32], h3(s_t1)[:, :, 32:64],
                                         s_tbl[:, 64:96].unsqueeze(1).broadcast_to([128, 8, 32]), ALU.mult),
            reads=["t1b", "tbl"], writes=["t2b"])
        vop(p, lambda e: e.tensor_tensor(h3(s_t2)[:, :, 32:64], h3(s_t1)[:, :, 0:32],
                                         s_tbl[:, 96:128].unsqueeze(1).broadcast_to([128, 8, 32]), ALU.mult),
            reads=["t1b", "tbl"], writes=["t2b"])
        vop(p, lambda e: e.tensor_tensor(h3(s_t1), h3(s_t1),
                                         s_tbl[:, 0:64].unsqueeze(1).broadcast_to([128, 8, 64]), ALU.mult),
            reads=["t1b", "tbl"], writes=["t1b"])
        vop(p, lambda e: e.tensor_add(s_t1[:], s_t1[:], s_t2[:]), reads=["t1b", "t2b"], writes=["t1b"])
        vop(p, lambda e: e.tensor_copy(s_tmbf[:], s_t1[:]), reads=["t1b"], writes=["tmbf"])

    for j in range(5):
        c0 = j * 512
        for i in range(8):
            dma(p, "sync", s_x[i][:], f_in[i * 128:(i + 1) * 128, c0:c0 + 513], writes=[f"rx{i}"])
        if with_ln:
            for i in range(8):
                act(p, s_sq[i][:], s_x[i][:], AF.Square, reads=[f"rx{i}"], writes=[f"sq{i}"])
                vop(p, lambda e, i=i: e.tensor_copy(s_xbf[i][:], s_x[i][:]), reads=[f"rx{i}"], writes=[f"xbf{i}"])
            for c, n in ((0, 512), (512, 1)):
                for i in range(8):
                    mm(p, P_A[0:1, c:c + n], s_onesb[:], s_xbf[i][:, c:c + n], i == 0, i == 7,
                       reads=["onesb", f"xbf{i}"], writes=["PA"])
                for i in range(8):
                    mm(p, P_B[0:1, c:c + n], s_onesb[:], s_sq[i][:, c:c + n], i == 0, i == 7,
                       reads=["onesb", f"sq{i}"], writes=["PB"])
            vop(p, lambda e: e.tensor_scalar_mul(s_mu[:], P_A[0:1, 0:513], 1.0 / D), reads=["PA"], writes=["mu"])
            vop(p, lambda e: e.tensor_scalar_mul(s_rstd[:], P_B[0:1, 0:513], 1.0 / D), reads=["PB"], writes=["rstd"])
            vop(p, lambda e: e.tensor_mul(s_var[:], s_mu[:], s_mu[:]), reads=["mu"], writes=["var"])
            vop(p, lambda e: e.tensor_sub(s_var[:], s_rstd[:], s_var[:]), reads=["rstd", "var"], writes=["var"])
            act(p, s_rstd[:], s_var[:], AF.Sqrt, bias=s_cst[0:1, 0:1], reads=["var", "cst"], writes=["rstd"])
            vop(p, lambda e: e.reciprocal(s_rstd[:], s_rstd[:]), reads=["rstd"], writes=["rstd"])
            for c, n in ((0, 512), (512, 1)):
                mm(p, P_B[0:128, c:c + n], s_ones1[:], s_mu[0:1, c:c + n], True, True,
                   reads=["ones1", "mu"], writes=["PB"])
            for c, n in ((0, 512), (512, 1)):
                mm(p, P_A[0:128, c:c + n], s_ones1[:], s_rstd[0:1, c:c + n], True, True,
                   reads=["ones1", "rstd"], writes=["PA"])
            for i in range(8):
                vop(p, lambda e, i=i: e.tensor_sub(s_z[i][:], s_x[i][:], P_B[:, 0:513]),
                    reads=[f"rx{i}", "PB"], writes=[f"zz{i}"])
                vop(p, lambda e, i=i: e.tensor_mul(s_z[i][:], s_z[i][:], P_A[:, 0:513]),
                    reads=[f"zz{i}", "PA"], writes=[f"zz{i}"])
                vop(p, lambda e, i=i: e.tensor_copy(s_xbf[i][:], s_z[i][:]), reads=[f"zz{i}"], writes=[f"xbf{i}"])
            if j >= 1:
                for i in range(8):
                    dma(p, "gpsimd", o_z[i * 128:(i + 1) * 128, c0 - 512:c0], s_z[i][:, 1:513],
                        reads=[f"zz{i}"], writes=["zodram"], group="zow")
        else:
            for i in range(8):
                vop(p, lambda e, i=i: e.tensor_copy(s_xbf[i][:], s_x[i][:]), reads=[f"rx{i}"], writes=[f"xbf{i}"])
        for mo in range(12):
            pp, ppn = P_C[mo % 2], f"PC{mo % 2}"
            for k in range(8):
                view = s_xbf[k][:, 1:513] if k < 4 else s_xbf[k][:, 0:512]
                mm(p, pp[:], s_wq[k][:, mo * 128:(mo + 1) * 128], view, k == 0, k == 7,
                   reads=[f"wq{k}", f"xbf{k}"], writes=[ppn])
            act(p, s_qkv[mo][:], pp[:], AF.Copy, reads=[ppn], writes=[f"qkv{mo}"])
        for c in range(4):
            jc = j * 4 + c
            for r in range(4):
                p.fw.op("tensor", lambda e, r=r, c=c: e.transpose(P_T[:], s_qkv[8 + r][:, c * 128:(c + 1) * 128], s_iden[:]),
                        reads=[f"qkv{8 + r}", "iden"], writes=["PT"])
                act(p, s_vtm[jc][:, r * 128:(r + 1) * 128], P_T[:], AF.Copy, reads=["PT"], writes=[f"vtm{jc}"])
            for r in range(4):
                p.fw.op("tensor", lambda e, r=r, c=c: e.transpose(P_T[:], s_qkv[4 + r][:, c * 128:(c + 1) * 128], s_iden[:]),
                        reads=[f"qkv{4 + r}", "iden"], writes=["PT"])
                act(p, s_tm[:, r * 128:(r + 1) * 128], P_T[:], AF.Copy, reads=["PT"], writes=["tm"])
            tmchunk_rotary(jc, False)
            for r in range(4):
                p.fw.op("tensor", lambda e, r=r: e.transpose(P_T[:], s_tmbf[:, r * 128:(r + 1) * 128], s_iden[:]),
                        reads=["tmbf", "iden"], writes=["PT"])
                act(p, s_k2[r][:, jc * 128:(jc + 1) * 128], P_T[:], AF.Copy, reads=["PT"], writes=[f"k2_{r}"])
            if j >= 1:
                for r in range(4):
                    p.fw.op("tensor", lambda e, r=r, c=c: e.transpose(P_T[:], s_qkv[r][:, c * 128:(c + 1) * 128], s_iden[:]),
                            reads=[f"qkv{r}", "iden"], writes=["PT"])
                    act(p, s_tm[:, r * 128:(r + 1) * 128], P_T[:], AF.Copy, reads=["PT"], writes=["tm"])
                tmchunk_rotary(jc, True)
                oc = jc * 128 - 512
                for r in range(4):
                    p.fw.op("tensor", lambda e, r=r: e.transpose(P_T[:], s_tmbf[:, r * 128:(r + 1) * 128], s_iden[:]),
                            reads=["tmbf", "iden"], writes=["PT"])
                    act(p, s_q2[r][:, oc:oc + 128], P_T[:], AF.Copy, reads=["PT"], writes=[f"q2_{r}"])
    for w in range(4):
        dma(p, "gpsimd", s_msk[:], amask[w, :, :], writes=["mskb"])
        for qt in range(4):
            for h in range(8):
                r, hh = h // 2, (h % 2) * 64
                pp, ppn = (P_A, "PA") if h % 2 == 0 else (P_B, "PB")
                for half in range(2):
                    mm(p, pp[:, half * 512:(half + 1) * 512],
                       s_q2[r][hh:hh + 64, w * 512 + qt * 128:w * 512 + (qt + 1) * 128],
                       s_k2[r][hh:hh + 64, w * 512 + half * 512:w * 512 + (half + 1) * 512],
                       True, True, reads=[f"q2_{r}", f"k2_{r}"], writes=[ppn])
                vop(p, lambda e, pp=pp, qt=qt: e.tensor_add(s_scs[:], pp[:], s_msk[:, qt * 1024:(qt + 1) * 1024]),
                    reads=[ppn, "mskb"], writes=["scs"])
                vop(p, lambda e: e.reduce_max(s_nmx[:, 0:1], s_scs[:], AX.X, negate=True),
                    reads=["scs"], writes=["nmx"])
                act(p, s_abf[:], s_scs[:], AF.Exp, bias=s_nmx[:, 0:1], accum_out=s_rs[:, 0:1],
                    reads=["scs", "nmx"], writes=["abf", "rsum"])
                vop(p, lambda e: e.reciprocal(s_rs[:], s_rs[:]), reads=["rsum"], writes=["rsum"])
                for kc in range(8):
                    p.fw.op("tensor", lambda e, kc=kc: e.transpose(P_T[:], s_abf[:, kc * 128:(kc + 1) * 128], s_iden[:]),
                            reads=["abf", "iden"], writes=["PT"])
                    act(p, s_aT[kc][:], P_T[:], AF.Copy, reads=["PT"], writes=[f"aT{kc}"])
                for kc in range(8):
                    mm(p, P_X[:, 0:64], s_aT[kc][:], s_vtm[4 * w + kc][:, h * 64:(h + 1) * 64],
                       kc == 0, kc == 7, reads=[f"aT{kc}", f"vtm{4 * w + kc}"], writes=["PX"])
                vop(p, lambda e, h=h: e.tensor_scalar(s_otm[:, h * 64:(h + 1) * 64], P_X[:, 0:64], s_rs[:, 0:1], None, ALU.mult),
                    reads=["PX", "rsum"], writes=["otm"])
            vop(p, lambda e: e.tensor_copy(s_tmbf[:], s_otm[:]), reads=["otm"], writes=["tmbf"])
            for r in range(4):
                p.fw.op("tensor", lambda e, r=r: e.transpose(P_T[:], s_tmbf[:, r * 128:(r + 1) * 128], s_iden[:]),
                        reads=["tmbf", "iden"], writes=["PT"])
                act(p, s_ofm[r][:, w * 512 + qt * 128:w * 512 + (qt + 1) * 128], P_T[:], AF.Copy,
                    reads=["PT"], writes=[f"ofm{r}"])
    rsrc = o_z if with_ln else f_in
    roff = 0 if with_ln else 513
    rnm = "zo_r" if with_ln else "f_r"
    for w in range(4):
        c0 = w * 512
        for m in range(8):
            pp, ppn = P_C[m % 2], f"PC{m % 2}"
            for r in range(4):
                mm(p, pp[:], s_wo[r][:, m * 128:(m + 1) * 128], s_ofm[r][:, c0:c0 + 512],
                   r == 0, r == 3, reads=[f"swo{r}", f"ofm{r}"], writes=[ppn])
            dma(p, "sync", s_fr[:], rsrc[m * 128:(m + 1) * 128, roff + c0:roff + c0 + 512],
                 reads=["zodram"] if with_ln else [], writes=["fr"], group="frl")
            vop(p, lambda e, pp=pp: e.scalar_tensor_tensor(s_fr[:], pp[:], 1.0, s_fr[:], ALU.mult, ALU.add),
                reads=[ppn, "fr"], writes=["fr"])
            dma(p, "gpsimd", o_f[m * 128:(m + 1) * 128, c0:c0 + 512], s_fr[:], reads=["fr"], group="fow")
    return p


def _rope_tables():
    n = SEQ
    dh = 64
    half = np.arange(0, dh, 2, dtype=np.float64)
    inv = 1.0 / 10000.0 ** (half / dh)
    t = np.arange(n, dtype=np.float64)
    fr = t[:, None] * inv[None, :]
    fr = np.concatenate([fr, fr], axis=-1)
    base = (half + 0.4 * dh) / (1.4 * dh)
    power = (t - n // 2) / 256.0
    sc = base[None, :] ** power[:, None]
    sc = np.concatenate([sc, sc], axis=-1)
    cos, sin = np.cos(fr), np.sin(fr)
    sgn = np.ones(dh)
    sgn[0:32] = -1.0
    cq = (8.0 * cos * sc).astype(np.float32)
    sq = (8.0 * sin * sc * sgn).astype(np.float32)
    ckk = (cos / sc).astype(np.float32)
    skk = (sin / sc * sgn).astype(np.float32)
    return cq, sq, ckk, skk


def _l1f_masks():
    NEG = np.float32(-3.3895314e38)
    qi = np.arange(512)
    out = np.zeros((2, 512, 1024), np.float32)
    for var in range(2):
        i = 512 + qi[:, None]
        j = np.arange(1024)[None, :]
        bad = (i < j) | (i > j + 512)
        if var == 1:
            bad = bad | (j < 512)
        out[var] = np.where(bad, NEG, 0.0)
    import ml_dtypes
    return out.astype(ml_dtypes.bfloat16)


def _prep_s6_maps(inputs, layer, f_full, with_ln):
    wqkv = np.ascontiguousarray(np.asarray(inputs["L1_ffn_wqkv"], np.float32)[layer])
    wov = np.ascontiguousarray(np.asarray(inputs["L1_ffn_wo"], np.float32)[layer])
    qs = np.asarray(inputs["L1_ffn_q_scale"], np.float32)[layer].reshape(1, 64)
    ks = np.asarray(inputs["L1_ffn_k_scale"], np.float32)[layer].reshape(1, 64)
    cq, sq, ckk, skk = _rope_tables()
    msk = _l1f_masks()
    cstv = np.zeros((128, 8), np.float32)
    cstv[:, 0] = 1e-5
    iden = np.eye(128, dtype=np.float32)
    fpad = np.pad(f_full, ((0, 0), (513, 0)))
    cqp = np.pad(cq, ((512, 0), (0, 0)))
    sqp = np.pad(sq, ((512, 0), (0, 0)))
    ckp = np.pad(ckk, ((512, 0), (0, 0)))
    skp = np.pad(skk, ((512, 0), (0, 0)))
    maps = []
    for c in range(NCORES):
        t0 = c * TC
        am = np.stack([msk[1 if (c == 0 and w == 0) else 0]
                       .reshape(4, 128, 1024).transpose(1, 0, 2).reshape(128, 4096)
                       for w in range(4)])
        am = np.ascontiguousarray(am)
        maps.append({
            "f": np.ascontiguousarray(fpad[:, t0:t0 + TH + 1]),
            "wqkv": wqkv, "wo": wov, "qsr": qs, "ksr": ks,
            "cq8": np.ascontiguousarray(cqp[t0:t0 + TH]),
            "snq": np.ascontiguousarray(sqp[t0:t0 + TH]),
            "ck": np.ascontiguousarray(ckp[t0:t0 + TH]),
            "snk": np.ascontiguousarray(skp[t0:t0 + TH]),
            "amask": am, "iden": iden, "cst": cstv,
        })
    return maps


def run_s6(inputs, layer, f_full, with_ln):
    p = build_s6(with_ln)
    nc = p.finish()
    maps = _prep_s6_maps(inputs, layer, f_full, with_ln)
    res = run_bass_kernel_spmd(nc, maps, core_ids=list(range(NCORES)))
    fo = np.concatenate([r["fo"] for r in res.results], axis=1)
    if with_ln:
        zo = np.concatenate([r["zo"] for r in res.results], axis=1)
        return fo, zo
    return fo, None


HID = 2730

def build_s7(final):
    p = Prog()
    f_in = p.dram("f", [D, TC + 1])
    w1 = p.dram("w1", [D, 2 * HID])
    w2 = p.dram("w2", [HID, D])
    prm = p.dram("prm", [128, 24])
    b1a = p.dram("b1a", [128, 22])
    b1g = p.dram("b1g", [128, 22])
    cst = p.dram("cst", [128, 8])
    if final:
        x3r = p.dram("x3r", [D, TC])
    o_f = p.dram("fo", [D, TC], kind="ExternalOutput")

    KT = [128] * 21 + [42]
    s_w1 = [p.sb(f"sw1_{k}", [128, 2 * HID], BF16) for k in range(8)]
    s_w2 = [p.sb(f"sw2_{k}", [128, D], BF16) for k in range(22)]
    for k in range(8):
        dma(p, "gpsimd", s_w1[k][:], w1[k * 128:(k + 1) * 128, :], writes=[f"sw1_{k}"], group="wl")
    for k in range(22):
        kw = KT[k]
        dma(p, "gpsimd", s_w2[k][0:kw, :], w2[k * 128:k * 128 + kw, :], writes=[f"sw2_{k}"], group="wl")
    p.fw.settle("wl", [f"sw1_{k}" for k in range(8)] + [f"sw2_{k}" for k in range(22)])
    s_prm = p.sb("c_prm", [128, 24])
    s_b1a = p.sb("c_b1a", [128, 22])
    s_b1g = p.sb("c_b1g", [128, 22])
    s_cst = p.sb("c_cst", [128, 8])
    s_ones1 = p.sb("ones1", [1, 128])
    s_onesb = p.sb("onesb", [128, 1], BF16)
    dma(p, "sync", s_prm[:], prm, writes=["prm"])
    dma(p, "sync", s_b1a[:], b1a, writes=["b1a"])
    dma(p, "sync", s_b1g[:], b1g, writes=["b1g"])
    dma(p, "sync", s_cst[:], cst, writes=["cst"])
    vop(p, lambda e: e.memset(s_ones1[:], 1.0), writes=["ones1"])
    vop(p, lambda e: e.memset(s_onesb[:], 1.0), writes=["onesb"])

    s_f = [p.sb(f"rf{i}", [128, 513]) for i in range(8)]
    s_fbf = [p.sb(f"fbf{i}", [128, 513], BF16) for i in range(8)]
    s_sqb = [p.sb(f"sqb{i}", [128, 513], BF16) for i in range(8)]
    s_xln = [p.sb(f"xlb{i}", [128, W], BF16) for i in range(8)]
    s_h2 = [p.sb(f"h2_{o}", [128, W], BF16) for o in range(22)]
    s_gl = p.sb("gl", [128, W], BF16)
    s_t1 = p.sb("t1s", [128, W])
    s_xr = p.sb("xrs", [128, W])
    s_mu = p.sb("mu", [1, W])
    s_var = p.sb("var", [1, W])
    s_rstd = p.sb("rstd", [1, W])
    P_sa = p.ps("Psa", [128, W])
    P_sb = p.ps("Psb", [128, W])
    P_mu = p.ps("Pmu", [128, W])
    P_rs = p.ps("Prs", [128, W])
    P_a = [p.ps("Pa0", [128, W]), p.ps("Pa1", [128, W])]
    P_g = [p.ps("Pg0", [128, W]), p.ps("Pg1", [128, W])]

    def shv(buf, i):
        return buf[:, 1:513] if i < 4 else buf[:, 0:512]

    for tt in range(NT):
        c0 = tt * W
        for i in range(8):
            dma(p, "sync", s_f[i][:], f_in[i * 128:(i + 1) * 128, c0:c0 + 513], writes=[f"rf{i}"])
            act(p, s_sqb[i][:], s_f[i][:], AF.Square, reads=[f"rf{i}"], writes=[f"sqb{i}"])
            vop(p, lambda e, i=i: e.tensor_copy(s_fbf[i][:], s_f[i][:]), reads=[f"rf{i}"], writes=[f"fbf{i}"])
        for i in range(8):
            mm(p, P_sa[0:1, 0:W], s_onesb[:], shv(s_fbf[i], i), i == 0, i == 7,
               reads=["onesb", f"fbf{i}"], writes=["Psa"])
        for i in range(8):
            mm(p, P_sb[0:1, 0:W], s_onesb[:], shv(s_sqb[i], i), i == 0, i == 7,
               reads=["onesb", f"sqb{i}"], writes=["Psb"])
        vop(p, lambda e: e.tensor_scalar_mul(s_mu[:], P_sa[0:1, 0:W], 1.0 / D), reads=["Psa"], writes=["mu"])
        vop(p, lambda e: e.tensor_scalar_mul(s_rstd[:], P_sb[0:1, 0:W], 1.0 / D), reads=["Psb"], writes=["rstd"])
        vop(p, lambda e: e.tensor_mul(s_var[:], s_mu[:], s_mu[:]), reads=["mu"], writes=["var"])
        vop(p, lambda e: e.tensor_sub(s_var[:], s_rstd[:], s_var[:]), reads=["rstd", "var"], writes=["var"])
        act(p, s_rstd[:], s_var[:], AF.Sqrt, bias=s_cst[0:1, 0:1], reads=["var", "cst"], writes=["rstd"])
        vop(p, lambda e: e.reciprocal(s_rstd[:], s_rstd[:]), reads=["rstd"], writes=["rstd"])
        mm(p, P_mu[0:128, 0:W], s_ones1[:], s_mu[0:1, 0:W], True, True, reads=["ones1", "mu"], writes=["Pmu"])
        mm(p, P_rs[0:128, 0:W], s_ones1[:], s_rstd[0:1, 0:W], True, True, reads=["ones1", "rstd"], writes=["Prs"])
        for i in range(8):
            vop(p, lambda e, i=i: e.scalar_tensor_tensor(s_t1[:], P_mu[:], -1.0, shv(s_f[i], i), ALU.mult, ALU.add),
                reads=["Pmu", f"rf{i}"], writes=["t1s"])
            vop(p, lambda e: e.tensor_mul(s_t1[:], s_t1[:], P_rs[:]), reads=["t1s", "Prs"], writes=["t1s"])
            vop(p, lambda e, i=i: e.tensor_scalar(s_xln[i][:], s_t1[:], s_prm[:, i:i + 1], s_prm[:, 8 + i:9 + i], ALU.mult, ALU.add),
                reads=["t1s", "prm"], writes=[f"xlb{i}"])
        for m in range(22):
            mw = KT[m]
            pa, pan = P_a[m % 2], f"Pa{m % 2}"
            pg, pgn = P_g[m % 2], f"Pg{m % 2}"
            for k in range(8):
                mm(p, pa[0:mw, :], s_w1[k][:, m * 128:m * 128 + mw], s_xln[k][:], k == 0, k == 7,
                   reads=[f"sw1_{k}", f"xlb{k}"], writes=[pan])
            for k in range(8):
                mm(p, pg[0:mw, :], s_w1[k][:, HID + m * 128:HID + m * 128 + mw], s_xln[k][:], k == 0, k == 7,
                   reads=[f"sw1_{k}", f"xlb{k}"], writes=[pgn])
            act(p, s_gl[0:mw, :], pg[0:mw, :], AF.Gelu, bias=s_b1g[0:mw, m:m + 1],
                reads=[pgn, "b1g"], writes=["gl"])
            vop(p, lambda e, m=m, mw=mw, pa=pa: e.scalar_tensor_tensor(s_h2[m][0:mw, :], pa[0:mw, :], s_b1a[0:mw, m:m + 1], s_gl[0:mw, :], ALU.add, ALU.mult),
                reads=[pan, "b1a", "gl"], writes=[f"h2_{m}"])
        for m in range(8):
            pa, pan = P_a[m % 2], f"Pa{m % 2}"
            for ko in range(22):
                mm(p, pa[:], s_w2[ko][0:KT[ko], m * 128:(m + 1) * 128], s_h2[ko][0:KT[ko], :],
                   ko == 0, ko == 21, reads=[f"sw2_{ko}", f"h2_{ko}"], writes=[pan])
            vop(p, lambda e, m=m, pa=pa: e.scalar_tensor_tensor(s_xr[:], pa[:], s_prm[:, 16 + m:17 + m], s_f[m][:, 1:513], ALU.add, ALU.add),
                reads=[pan, "prm", f"rf{m}"], writes=["xrs"])
            if final:
                dma(p, "sync", s_t1[:], x3r[m * 128:(m + 1) * 128, c0:c0 + W], writes=["t1s"])
                vop(p, lambda e: e.tensor_add(s_xr[:], s_xr[:], s_t1[:]), reads=["xrs", "t1s"], writes=["xrs"])
            dma(p, "gpsimd", o_f[m * 128:(m + 1) * 128, c0:c0 + W], s_xr[:], reads=["xrs"], group="fow")
    return p


def _prep_s7_maps(inputs, layer, f_full, final, x3_full=None):
    w1 = np.ascontiguousarray(np.asarray(inputs["L1_ffn_w1"], np.float32)[layer])
    w2 = np.ascontiguousarray(np.asarray(inputs["L1_ffn_w2"], np.float32)[layer])
    b1 = np.asarray(inputs["L1_ffn_b1"], np.float32)[layer]
    b1ac = np.zeros((128, 22), np.float32)
    b1gc = np.zeros((128, 22), np.float32)
    b1ap = np.pad(b1[:HID], (0, 22 * 128 - HID)).reshape(22, 128).T
    b1gp = np.pad(b1[HID:], (0, 22 * 128 - HID)).reshape(22, 128).T
    b1ac[:, :] = b1ap
    b1gc[:, :] = b1gp
    prm = np.zeros((128, 24), np.float32)
    prm[:, 0:8] = np.asarray(inputs["L1_ffn_ln_g"], np.float32)[layer].reshape(8, 128).T
    prm[:, 8:16] = np.asarray(inputs["L1_ffn_ln_b"], np.float32)[layer].reshape(8, 128).T
    prm[:, 16:24] = np.asarray(inputs["L1_ffn_b2"], np.float32)[layer].reshape(8, 128).T
    cstv = np.zeros((128, 8), np.float32)
    cstv[:, 0] = 1e-5
    fpad = np.pad(f_full, ((0, 0), (1, 0)))
    maps = []
    for c in range(NCORES):
        t0 = c * TC
        mp = {
            "f": np.ascontiguousarray(fpad[:, t0:t0 + TC + 1]),
            "w1": w1, "w2": w2, "prm": prm, "b1a": b1ac, "b1g": b1gc, "cst": cstv,
        }
        if final:
            mp["x3r"] = np.ascontiguousarray(x3_full[:, t0:t0 + TC])
        maps.append(mp)
    return maps


def run_s7(inputs, layer, f_full, final, x3_full=None):
    p = build_s7(final)
    nc = p.finish()
    maps = _prep_s7_maps(inputs, layer, f_full, final, x3_full)
    res = run_bass_kernel_spmd(nc, maps, core_ids=list(range(NCORES)))
    return np.concatenate([r["fo"] for r in res.results], axis=1)


def kernel(**inputs):
    xT = np.ascontiguousarray(np.asarray(inputs["L0_attn_x"], np.float32)[0].T)
    p = build_s1()
    nc = p.finish()
    res = run_bass_kernel_spmd(nc, _prep_s1_maps(inputs), core_ids=list(range(NCORES)))
    xln = np.concatenate([r["xln"] for r in res.results], axis=1)
    u = np.concatenate([r["u"] for r in res.results], axis=1)
    dt = np.concatenate([r["dt"] for r in res.results], axis=1)
    y = run_s2(inputs, dt, u)
    yf = np.zeros((D, SEQ), np.float32)
    for c in range(NCORES):
        yf[c * 128:(c + 1) * 128] = y[c * 128:(c + 1) * 128]
    x1 = run_s3(inputs, xT, xln, yf)
    x2 = run_s4(inputs, x1)
    x3 = run_s5(inputs, x2)
    f, _ = run_s6(inputs, 0, x3, True)
    f = run_s7(inputs, 0, f, False)
    f, _ = run_s6(inputs, 1, f, False)
    out = run_s7(inputs, 1, f, True, x3)
    return np.ascontiguousarray(out.T)[None].astype(np.float32)


# version 02
# speedup vs baseline: 1.2623x; 1.2623x over previous
import numpy as np
from contextlib import ExitStack

import concourse.bass as bass
import concourse.mybir as mybir
from concourse.bass_utils import run_bass_kernel_spmd

F32 = mybir.dt.float32
BF16 = mybir.dt.bfloat16
AF = mybir.ActivationFunctionType
ALU = mybir.AluOpType
AX = mybir.AxisListType

NCORES = 8
D = 1024
SEQ = 16384
TC = SEQ // NCORES
W = 512
NT = TC // W
NH = 16
DH = 64
DS = 48
DTR = 64


class _Res:
    __slots__ = ("name", "writer", "readers")

    def __init__(self, name):
        self.name = name
        self.writer = None
        self.readers = []


class Fw:
    ENG = ("sync", "scalar", "tensor", "vector", "gpsimd")

    def __init__(self, nc):
        self.nc = nc
        self.ops = {e: [] for e in self.ENG}
        self.cnt = {e: 0 for e in self.ENG}
        self._res = {}

    def res(self, name):
        r = self._res.get(name)
        if r is None:
            r = self._res[name] = _Res(name)
        return r

    def tok(self, eng):
        return (eng, self.cnt[eng])

    def op(self, eng, emit, reads=(), writes=(), dma=False, light=False, after=(), group=None):
        deps = {}

        def need(t):
            if t is None:
                return
            e, v = t
            if light and e == eng:
                return
            if deps.get(e, 0) < v:
                deps[e] = v

        rs = [self.res(x) if isinstance(x, str) else x for x in reads]
        ws = [self.res(x) if isinstance(x, str) else x for x in writes]
        for r in rs:
            need(r.writer)
        for w in ws:
            need(w.writer)
            for t in w.readers:
                need(t)
        for t in after:
            need(t)
        if dma:
            if group is None:
                group = ws[0].name if ws else rs[0].name
            key = "dma:" + group
            inc = 16
        else:
            key = eng
            inc = 1
        myv = self.cnt.get(key, 0) + inc
        self.ops[eng].append((emit, deps, inc, key, (myv - inc) if dma else None))
        self.cnt[key] = myv
        t = (key, myv)
        for w in ws:
            w.writer = t
            w.readers = []
        for r in rs:
            r.readers.append(t)
        return t

    def settle(self, group, names):
        key = "dma:" + group
        t = (key, self.cnt[key])
        for n in names:
            self.res(n).writer = t

    def build(self, blk, sems):
        final = dict(self.cnt)

        for e in self.ENG:
            ops = self.ops[e]
            if not ops and e != "sync":
                continue

            def driver(eng, e=e, ops=ops):
                waited = {}
                last_dma = False
                for emit, deps, inc, key, prior in ops:
                    if prior is not None and prior > 0:
                        if deps.get(key, 0) < prior:
                            deps[key] = prior
                    for de, dv in deps.items():
                        if waited.get(de, 0) < dv:
                            eng.wait_ge(sems[de], dv)
                            waited[de] = dv
                    inst = emit(eng)
                    inst.then_inc(sems[key], inc)
                if e == "sync":
                    for k, v in final.items():
                        if v > 0 and waited.get(k, 0) < v:
                            eng.wait_ge(sems[k], v)

            getattr(blk, e)(driver)


class Prog:
    def __init__(self):
        self.nc = bass.Bass("TRN2", target_bir_lowering=False)
        self.fw = Fw(self.nc)
        self.stack = ExitStack()

    def dram(self, name, shape, dtype=F32, kind="ExternalInput"):
        return self.nc.dram_tensor(name, list(shape), dtype, kind=kind).ap()

    def sb(self, name, shape, dtype=F32):
        return self.stack.enter_context(self.nc.sbuf_tensor(name, list(shape), dtype))

    def ps(self, name, shape, dtype=F32):
        return self.stack.enter_context(self.nc.psum_tensor(name, list(shape), dtype))

    def finish(self):
        nc, fw = self.nc, self.fw
        sems = {}
        for k in fw.cnt:
            sems[k] = self.stack.enter_context(nc.semaphore(f"sem_{k}".replace(":", "_")))
        blk = self.stack.enter_context(nc.Block())
        fw.build(blk, sems)
        self.stack.close()
        return nc


def dma(p, q, out, in_, reads=(), writes=(), group=None, **kw):
    return p.fw.op(q, lambda e: e.dma_start(out=out, in_=in_, **kw), reads=reads, writes=writes,
                   dma=True, group=group)


def mm(p, out, lhsT, rhs, start, stop, reads=(), writes=()):
    return p.fw.op("tensor", lambda e: e.matmul(out, lhsT, rhs, start=start, stop=stop),
                   reads=reads, writes=writes, light=True)


def act(p, out, in_, func, bias=0.0, scale=1.0, accum_out=None, reads=(), writes=()):
    def f(e):
        if accum_out is not None:
            return e.activation(out, in_, func, bias=bias, scale=scale, accum_out=accum_out)
        return e.activation(out, in_, func, bias=bias, scale=scale)
    return p.fw.op("scalar", f, reads=reads, writes=writes)


def vop(p, emit, reads=(), writes=(), eng="vector"):
    return p.fw.op(eng, emit, reads=reads, writes=writes)


def build_s1():
    p = Prog()
    WH = W + 3

    xT = p.dram("xT", [D, TC + 3])
    pw_wT = p.dram("pw_wT", [D, D])
    prm = p.dram("prm", [128, 8 * 4 + 8 * 3 + 1])
    xpj_bd = p.dram("xpj_bd", [128, 128])
    dtpj_bd = p.dram("dtpj_bd", [128, 128])
    cst = p.dram("cst", [128, 8])

    o_xln = p.dram("xln", [D, TC], kind="ExternalOutput")
    o_u = p.dram("u", [D, TC], kind="ExternalOutput")
    o_dt = p.dram("dt", [D, TC], kind="ExternalOutput")

    s_pwT = [p.sb(f"pwT{k}", [128, D], BF16) for k in range(8)]
    s_xpj = p.sb("xpj", [128, 128], BF16)
    s_dtpj = p.sb("dtpj", [128, 128], BF16)
    s_prm = p.sb("c_prm", [128, 57])
    s_ones1 = p.sb("ones1", [1, 128])
    s_onesb = p.sb("onesb", [128, 1], BF16)
    s_cst = p.sb("c_cst", [128, 8])

    for k in range(8):
        dma(p, "gpsimd", s_pwT[k][:], pw_wT[k * 128:(k + 1) * 128, :], writes=[f"pwT{k}"])
    dma(p, "gpsimd", s_xpj[:], xpj_bd, writes=["xpj"])
    dma(p, "gpsimd", s_dtpj[:], dtpj_bd, writes=["dtpj"])
    dma(p, "sync", s_prm[:], prm, writes=["prm"])
    dma(p, "sync", s_cst[:], cst, writes=["cst"])
    vop(p, lambda e: e.memset(s_ones1[:], 1.0), writes=["ones1"])
    vop(p, lambda e: e.memset(s_onesb[:], 1.0), writes=["onesb"])
    DWK = 0
    DWB = 32
    PWB = 40
    N1W = 48
    DTB = 56

    s_x = [p.sb(f"x{i}", [128, WH]) for i in range(8)]
    s_xbf = [p.sb(f"xbf{i}", [128, WH], BF16) for i in range(8)]
    s_xsq = [p.sb(f"xsq{i}", [128, WH], BF16) for i in range(8)]
    s_t1 = [p.sb(f"t1_{i}", [128, WH]) for i in range(8)]
    s_xln = [p.sb(f"xln{i}", [128, W]) for i in range(8)]
    s_u = [p.sb(f"u{i}", [128, WH]) for i in range(8)]
    s_ucbf = [p.sb(f"ucbf{i}", [128, W], BF16) for i in range(8)]
    s_xh = [p.sb(f"xh{i}", [128, W]) for i in range(8)]
    s_xhbf = [p.sb(f"xhbf{i}", [128, W], BF16) for i in range(8)]
    s_dtxbf = p.sb("dtxbf", [128, W], BF16)
    s_sv = p.sb("sv", [128, 8 * W])
    s_sp1 = p.sb("sp1", [128, 8 * W])
    s_sp2 = p.sb("sp2", [128, 8 * W])

    s_mu = p.sb("mu", [1, WH])
    s_ex2 = p.sb("ex2", [1, WH])
    s_var = p.sb("var", [1, WH])
    s_rstd = p.sb("rstd", [1, WH])
    s_ra = p.sb("ra", [1, WH])
    s_rowc = p.sb("rowc", [1, WH])
    s_mub = p.sb("mub", [128, WH])
    s_rstdb = p.sb("rstdb", [128, WH])
    s_rowcb = p.sb("rowcb", [128, WH])

    P_stx = p.ps("Pstx", [128, 1024])
    P_stq = p.ps("Pstq", [128, 1024])
    P_pw = [p.ps("Ppw0", [128, W]), p.ps("Ppw1", [128, W])]
    P_xp = p.ps("Pxp", [128, W])

    def bcast(dst_ps, psname, row_ap, row_name, ncols):
        c = 0
        while c < ncols:
            n = min(512, ncols - c)
            mm(p, dst_ps[0:128, c:c + n], s_ones1[:], row_ap[0:1, c:c + n], True, True,
               reads=["ones1", row_name], writes=[psname])
            c += n

    for tt in range(NT):
        c0 = tt * W
        for i in range(8):
            dma(p, "sync", s_x[i][:], xT[i * 128:(i + 1) * 128, c0:c0 + WH], writes=[f"x{i}"])
        for i in range(8):
            act(p, s_xsq[i][:], s_x[i][:], AF.Square, reads=[f"x{i}"], writes=[f"xsq{i}"])
            vop(p, lambda e, i=i: e.tensor_copy(s_xbf[i][:], s_x[i][:]),
                reads=[f"x{i}"], writes=[f"xbf{i}"])
        for c, n in ((0, 512), (512, 3)):
            for i in range(8):
                mm(p, P_stx[0:1, c:c + n], s_onesb[:], s_xbf[i][:, c:c + n], i == 0, i == 7,
                   reads=["onesb", f"xbf{i}"], writes=["Pstx"])
            for i in range(8):
                mm(p, P_stq[0:1, c:c + n], s_onesb[:], s_xsq[i][:, c:c + n], i == 0, i == 7,
                   reads=["onesb", f"xsq{i}"], writes=["Pstq"])
        vop(p, lambda e: e.tensor_scalar_mul(s_mu[:], P_stx[0:1, 0:WH], 1.0 / D), reads=["Pstx"], writes=["mu"])
        vop(p, lambda e: e.tensor_scalar_mul(s_ex2[:], P_stq[0:1, 0:WH], 1.0 / D), reads=["Pstq"], writes=["ex2"])
        vop(p, lambda e: e.tensor_mul(s_var[:], s_mu[:], s_mu[:]), reads=["mu"], writes=["var"])
        vop(p, lambda e: e.tensor_sub(s_var[:], s_ex2[:], s_var[:]), reads=["ex2", "var"], writes=["var"])
        act(p, s_rstd[:], s_var[:], AF.Sqrt, bias=s_cst[0:1, 0:1], reads=["var", "cst"], writes=["rstd"])
        vop(p, lambda e: e.reciprocal(s_rstd[:], s_rstd[:]), reads=["rstd"], writes=["rstd"])
        vop(p, lambda e: e.tensor_mul(s_ra[:], s_var[:], s_rstd[:]), reads=["var", "rstd"], writes=["ra"])
        vop(p, lambda e: e.tensor_mul(s_ra[:], s_ra[:], s_rstd[:]), reads=["ra", "rstd"], writes=["ra"])
        act(p, s_ra[:], s_ra[:], AF.Sqrt, bias=s_cst[0:1, 1:2], reads=["ra", "cst"], writes=["ra"])
        vop(p, lambda e: e.reciprocal(s_ra[:], s_ra[:]), reads=["ra"], writes=["ra"])
        vop(p, lambda e: e.tensor_mul(s_rowc[:], s_ra[:], s_rstd[:]), reads=["ra", "rstd"], writes=["rowc"])
        bcast(P_stx, "Pstx", s_mu, "mu", WH)
        act(p, s_mub[:], P_stx[0:128, 0:WH], AF.Copy, reads=["Pstx"], writes=["mub"])
        bcast(P_stq, "Pstq", s_rstd, "rstd", WH)
        act(p, s_rstdb[:], P_stq[0:128, 0:WH], AF.Copy, reads=["Pstq"], writes=["rstdb"])
        bcast(P_stx, "Pstx", s_rowc, "rowc", WH)
        act(p, s_rowcb[:], P_stx[0:128, 0:WH], AF.Copy, reads=["Pstx"], writes=["rowcb"])

        for i in range(8):
            vop(p, lambda e, i=i: e.tensor_sub(s_t1[i][:], s_x[i][:], s_mub[:]),
                reads=[f"x{i}", "mub"], writes=[f"t1{i}"])
            vop(p, lambda e, i=i: e.tensor_mul(s_xln[i][:], s_t1[i][:, 3:WH], s_rstdb[:, 3:WH]),
                reads=[f"t1{i}", "rstdb"], writes=[f"xln{i}"])
            vop(p, lambda e, i=i: e.scalar_tensor_tensor(s_u[i][:], s_t1[i][:], s_prm[:, N1W + i:N1W + i + 1], s_rowcb[:], ALU.mult, ALU.mult),
                reads=[f"t1{i}", "prm", "rowcb"], writes=[f"u{i}"])
            vop(p, lambda e, i=i: e.tensor_scalar(s_xh[i][:], s_u[i][:, 0:W], s_prm[:, DWK + 4 * i:DWK + 4 * i + 1], s_prm[:, DWB + i:DWB + i + 1], ALU.mult, ALU.add),
                reads=[f"u{i}", "prm"], writes=[f"xh{i}"])
            for j in (1, 2):
                vop(p, lambda e, i=i, j=j: e.scalar_tensor_tensor(s_xh[i][:], s_u[i][:, j:j + W], s_prm[:, DWK + 4 * i + j:DWK + 4 * i + j + 1], s_xh[i][:], ALU.mult, ALU.add),
                    reads=[f"u{i}", "prm", f"xh{i}"], writes=[f"xh{i}"])
            vop(p, lambda e, i=i: e.scalar_tensor_tensor(s_ucbf[i][:], s_u[i][:, 3:3 + W], s_prm[:, DWK + 4 * i + 3:DWK + 4 * i + 4], s_xh[i][:], ALU.mult, ALU.add),
                reads=[f"u{i}", "prm", f"xh{i}"], writes=[f"ucbf{i}"])
        for m in range(8):
            pp = P_pw[m % 2]
            ppn = f"Ppw{m % 2}"
            for k in range(8):
                mm(p, pp[:], s_pwT[k][:, m * 128:(m + 1) * 128], s_ucbf[k][:], k == 0, k == 7,
                   reads=[f"pwT{k}", f"ucbf{k}"], writes=[ppn])
            vop(p, lambda e, m=m, pp=pp: e.tensor_scalar(s_xh[m][:], pp[:], s_prm[:, PWB + m:PWB + m + 1], None, ALU.add),
                reads=[ppn, "prm"], writes=[f"xh{m}"])
            act(p, s_xhbf[m][:], s_xh[m][:], AF.Copy, reads=[f"xh{m}"], writes=[f"xhbf{m}"])
            dma(p, "gpsimd", o_u[m * 128:(m + 1) * 128, tt * W:(tt + 1) * W], s_xh[m][:], reads=[f"xh{m}"])
            dma(p, "gpsimd", o_xln[m * 128:(m + 1) * 128, tt * W:(tt + 1) * W], s_xln[m][:], reads=[f"xln{m}"])
        for hp in range(8):
            mm(p, P_xp[:], s_xpj[:], s_xhbf[hp][:], True, True,
               reads=["xpj", f"xhbf{hp}"], writes=["Pxp"])
            act(p, s_dtxbf[:], P_xp[:], AF.Copy, reads=["Pxp"], writes=["dtxbf"])
            mm(p, P_xp[:], s_dtpj[:], s_dtxbf[:], True, True,
               reads=["dtpj", "dtxbf"], writes=["Pxp"])
            vop(p, lambda e, hp=hp: e.tensor_scalar(s_sv[:, hp * W:(hp + 1) * W], P_xp[:], s_prm[:, DTB:DTB + 1], None, ALU.add),
                reads=["Pxp", "prm"], writes=["sv"])
        act(p, s_sp1[:], s_sv[:], AF.Abs, reads=["sv"], writes=["sp1"])
        act(p, s_sp1[:], s_sp1[:], AF.Exp, scale=s_cst[:, 3:4], reads=["sp1", "cst"], writes=["sp1"])
        act(p, s_sp1[:], s_sp1[:], AF.Ln, bias=s_cst[:, 2:3], reads=["sp1", "cst"], writes=["sp1"])
        vop(p, lambda e: e.tensor_scalar_max(s_sp2[:], s_sv[:], 0.0), reads=["sv"], writes=["sp2"])
        vop(p, lambda e: e.tensor_add(s_sp2[:], s_sp2[:], s_sp1[:]), reads=["sp1", "sp2"], writes=["sp2"])
        for hp in range(8):
            dma(p, "gpsimd", o_dt[hp * 128:(hp + 1) * 128, tt * W:(tt + 1) * W], s_sp2[:, hp * W:(hp + 1) * W],
                reads=["sp2"])
    return p


def _prep_s1_maps(inputs):
    xT = np.ascontiguousarray(inputs["L0_attn_x"][0].T).astype(np.float32)
    xTh = np.pad(xT, ((0, 0), (3, 0)))
    pw_wT = np.ascontiguousarray(inputs["L0_attn_pw_w"].T).astype(np.float32)
    xpw = np.asarray(inputs["L0_attn_x_proj_w"], np.float32)
    dtw = np.asarray(inputs["L0_attn_dt_proj_w"], np.float32)
    xpj = np.zeros((128, 128), np.float32)
    xpj[0:64, 0:64] = xpw[0:64].T
    xpj[64:128, 64:128] = xpw[0:64].T
    dtpj = np.zeros((128, 128), np.float32)
    dtpj[0:64, 0:64] = dtw.T
    dtpj[64:128, 64:128] = dtw.T
    prm = np.zeros((128, 57), np.float32)
    dwk = np.asarray(inputs["L0_attn_dw_kernel"], np.float32)[:, 0, :].reshape(8, 128, 4)
    for m in range(8):
        for j in range(4):
            prm[:, m * 4 + j] = dwk[m, :, j]
    prm[:, 32:40] = np.asarray(inputs["L0_attn_dw_bias"], np.float32).reshape(8, 128).T
    prm[:, 40:48] = np.asarray(inputs["L0_attn_pw_b"], np.float32).reshape(8, 128).T
    prm[:, 48:56] = np.asarray(inputs["L0_attn_norm1_w"], np.float32).reshape(8, 128).T
    prm[:, 56] = np.tile(np.asarray(inputs["L0_attn_dt_proj_b"], np.float32), 2)
    cstv = np.zeros((128, 8), np.float32)
    cstv[:, 0] = 1e-5
    cstv[:, 1] = 1e-6
    cstv[:, 2] = 1.0
    cstv[:, 3] = -1.0
    maps = []
    for c in range(NCORES):
        maps.append({
            "xT": np.ascontiguousarray(xTh[:, c * TC:c * TC + TC + 3]),
            "pw_wT": pw_wT,
            "prm": prm,
            "xpj_bd": xpj,
            "dtpj_bd": dtpj,
            "cst": cstv,
        })
    return maps


def run_s1(inputs):
    p = build_s1()
    nc = p.finish()
    maps = _prep_s1_maps(inputs)
    res = run_bass_kernel_spmd(nc, maps, core_ids=list(range(NCORES)))
    xln = np.concatenate([r["xln"] for r in res.results], axis=1)
    u = np.concatenate([r["u"] for r in res.results], axis=1)
    dt = np.concatenate([r["dt"] for r in res.results], axis=1)
    return xln, u, dt


def kernel(**inputs):
    raise NotImplementedError


S2T = 2048

def build_s2():
    p = Prog()
    NCH = SEQ // S2T

    dt = p.dram("dt", [128, SEQ])
    u = p.dram("u", [128, SEQ])
    wrep = p.dram("wrep", [96, 128, 128])
    alog = p.dram("alog", [128, DS])
    o_y = p.dram("y", [128, SEQ], kind="ExternalOutput")

    s_wrep = [p.sb(f"wr{j}", [128, 128], BF16) for j in range(96)]
    for j in range(96):
        dma(p, "gpsimd", s_wrep[j][:], wrep[j, :, :], writes=[f"wr{j}"], group="wrepall")
    p.fw.settle("wrepall", [f"wr{j}" for j in range(96)])
    s_alog = p.sb("c_alog", [128, DS])
    s_nega = p.sb("nega", [128, DS])
    dma(p, "sync", s_alog[:], alog, writes=["alog"])
    act(p, s_nega[:], s_alog[:], AF.Exp, reads=["alog"], writes=["nega"])
    vop(p, lambda e: e.tensor_scalar_mul(s_nega[:], s_nega[:], -1.0), reads=["nega"], writes=["nega"])
    s_state = p.sb("state", [128, DS])
    vop(p, lambda e: e.memset(s_state[:], 0.0), writes=[f"st{s}" for s in range(DS)])

    s_dt = [p.sb(f"dtc{j}", [128, S2T]) for j in range(2)]
    s_u = [p.sb(f"uc{j}", [128, S2T]) for j in range(2)]
    s_ubf = [p.sb(f"ubf{j}", [128, S2T], BF16) for j in range(2)]
    s_dtu = [p.sb(f"dtu{j}", [128, S2T]) for j in range(2)]
    s_a = [p.sb(f"ab{j}", [128, S2T]) for j in range(2)]
    s_xin = [p.sb(f"xin{j}", [128, S2T]) for j in range(2)]
    s_H = [p.sb(f"H{j}", [128, S2T]) for j in range(2)]
    s_hc = [p.sb(f"hc{j}", [128, S2T]) for j in range(2)]
    s_cev = p.sb("cev", [128, S2T])
    s_y = p.sb("yb", [128, S2T])
    P_B = p.ps("PB", [128, S2T])
    P_C = p.ps("PC", [128, S2T])

    for c in range(NCH):
        par = c % 2
        t0 = c * S2T
        dma(p, "sync", s_dt[par][:], dt[:, t0:t0 + S2T], writes=[f"dtc{par}"])
        dma(p, "sync", s_u[par][:], u[:, t0:t0 + S2T], writes=[f"uc{par}"])
        dma(p, "gpsimd", s_ubf[par][:], u[:, t0:t0 + S2T], writes=[f"ubf{par}"])
        vop(p, lambda e, par=par: e.tensor_mul(s_dtu[par][:], s_dt[par][:], s_u[par][:]),
            reads=[f"dtc{par}", f"uc{par}"], writes=[f"dtu{par}"])
        for s in range(DS):
            sp = s % 2
            for j in range(4):
                mm(p, P_B[:, j * 512:(j + 1) * 512], s_wrep[2 * s][:], s_ubf[par][:, j * 512:(j + 1) * 512],
                   True, True, reads=[f"wr{2 * s}", f"ubf{par}"], writes=["PB"])
                mm(p, P_C[:, j * 512:(j + 1) * 512], s_wrep[2 * s + 1][:], s_ubf[par][:, j * 512:(j + 1) * 512],
                   True, True, reads=[f"wr{2 * s + 1}", f"ubf{par}"], writes=["PC"])
            act(p, s_a[sp][:], s_dt[par][:], AF.Exp, scale=s_nega[:, s:s + 1],
                reads=[f"dtc{par}", "nega"], writes=[f"ab{sp}"])
            vop(p, lambda e, sp=sp, par=par: e.tensor_mul(s_xin[sp][:], s_dtu[par][:], P_B[:]),
                reads=[f"dtu{par}", "PB"], writes=[f"xin{sp}"])
            vop(p, lambda e, sp=sp, s=s: e.tensor_tensor_scan(s_H[sp][:], s_a[sp][:], s_xin[sp][:],
                                                              s_state[:, s:s + 1], ALU.mult, ALU.add),
                reads=[f"ab{sp}", f"xin{sp}", f"st{s}"], writes=[f"H{sp}"])
            vop(p, lambda e, sp=sp, s=s: e.tensor_copy(s_state[:, s:s + 1], s_H[sp][:, S2T - 1:S2T]),
                reads=[f"H{sp}"], writes=[f"st{s}"])
            if s % 2 == 0:
                act(p, s_cev[:], P_C[:], AF.Copy, reads=["PC"], writes=["cev"])
                vop(p, lambda e, sp=sp: e.tensor_mul(s_hc[sp][:], s_H[sp][:], s_cev[:]),
                    reads=[f"H{sp}", "cev"], writes=[f"hc{sp}"], eng="gpsimd")
            else:
                vop(p, lambda e, sp=sp: e.tensor_mul(s_hc[sp][:], s_H[sp][:], P_C[:]),
                    reads=[f"H{sp}", "PC"], writes=[f"hc{sp}"])
            if s == 0:
                vop(p, lambda e, sp=sp: e.tensor_copy(s_y[:], s_hc[sp][:]),
                    reads=[f"hc{sp}"], writes=["yb"], eng="gpsimd")
            else:
                vop(p, lambda e, sp=sp: e.tensor_add(s_y[:], s_y[:], s_hc[sp][:]),
                    reads=[f"hc{sp}", "yb"], writes=["yb"], eng="gpsimd")
        dma(p, "gpsimd", o_y[:, t0:t0 + S2T], s_y[:], reads=["yb"])
    return p


def _prep_s2_maps(inputs, dt_full, u_full):
    xpw = np.asarray(inputs["L0_attn_x_proj_w"], np.float32)
    alog = np.asarray(inputs["L0_attn_A_log"], np.float32)
    wrep = np.zeros((96, 128, 128), np.float32)
    for s in range(DS):
        rb = np.repeat(xpw[DTR + s][:, None], 64, axis=1)
        rc = np.repeat(xpw[DTR + DS + s][:, None], 64, axis=1)
        wrep[2 * s, 0:64, 0:64] = rb
        wrep[2 * s, 64:128, 64:128] = rb
        wrep[2 * s + 1, 0:64, 0:64] = rc
        wrep[2 * s + 1, 64:128, 64:128] = rc
    maps = []
    for c in range(NCORES):
        al = np.zeros((128, DS), np.float32)
        al[0:64] = alog[2 * c]
        al[64:128] = alog[2 * c + 1]
        maps.append({
            "dt": np.ascontiguousarray(dt_full[c * 128:(c + 1) * 128]),
            "u": np.ascontiguousarray(u_full[c * 128:(c + 1) * 128]),
            "wrep": wrep,
            "alog": al,
        })
    return maps


def run_s2(inputs, dt_full, u_full):
    p = build_s2()
    nc = p.finish()
    maps = _prep_s2_maps(inputs, dt_full, u_full)
    res = run_bass_kernel_spmd(nc, maps, core_ids=list(range(NCORES)))
    return np.concatenate([r["y"] for r in res.results], axis=0)


def build_s3():
    p = Prog()
    xT = p.dram("xT", [D, TC])
    xln = p.dram("xln", [D, TC])
    y = p.dram("y", [D, TC])
    opT = p.dram("opT", [D, D])
    gT = p.dram("gT", [D, 2 * D])
    uT = p.dram("uT", [D, 2 * D])
    dT = p.dram("dT", [2 * D, D])
    prm = p.dram("prm", [128, 16])
    cst = p.dram("cst", [128, 8])
    o_x1 = p.dram("x1", [D, TC], kind="ExternalOutput")

    s_opT = [p.sb(f"opT{k}", [128, D], BF16) for k in range(8)]
    s_gT = [p.sb(f"gT{k}", [128, 2 * D], BF16) for k in range(8)]
    s_uT = [p.sb(f"uT{k}", [128, 2 * D], BF16) for k in range(8)]
    s_dT = [p.sb(f"dT{k}", [128, D], BF16) for k in range(16)]
    for k in range(8):
        dma(p, "gpsimd", s_opT[k][:], opT[k * 128:(k + 1) * 128, :], writes=[f"opT{k}"], group="wload")
        dma(p, "gpsimd", s_gT[k][:], gT[k * 128:(k + 1) * 128, :], writes=[f"gT{k}"], group="wload")
        dma(p, "gpsimd", s_uT[k][:], uT[k * 128:(k + 1) * 128, :], writes=[f"uT{k}"], group="wload")
    for k in range(16):
        dma(p, "gpsimd", s_dT[k][:], dT[k * 128:(k + 1) * 128, :], writes=[f"dT{k}"], group="wload")
    p.fw.settle("wload", [f"opT{k}" for k in range(8)] + [f"gT{k}" for k in range(8)]
                + [f"uT{k}" for k in range(8)] + [f"dT{k}" for k in range(16)])
    s_prm = p.sb("c_prm", [128, 16])
    s_cst = p.sb("c_cst", [128, 8])
    s_ones1 = p.sb("ones1", [1, 128])
    s_onesb = p.sb("onesb", [128, 1], BF16)
    dma(p, "sync", s_prm[:], prm, writes=["prm"])
    dma(p, "sync", s_cst[:], cst, writes=["cst"])
    vop(p, lambda e: e.memset(s_ones1[:], 1.0), writes=["ones1"])
    vop(p, lambda e: e.memset(s_onesb[:], 1.0), writes=["onesb"])

    s_ybf = [p.sb(f"ybf{i}", [128, W], BF16) for i in range(8)]
    s_xln = [p.sb(f"xl{i}", [128, W]) for i in range(8)]
    s_x = [p.sb(f"rx{i}", [128, W]) for i in range(8)]
    s_x1 = [p.sb(f"sx1_{i}", [128, W]) for i in range(8)]
    s_t = p.sb("tshared", [128, W])
    s_h2 = [p.sb(f"h2_{i}", [128, W], BF16) for i in range(8)]
    s_sact = p.sb("sact", [128, W], BF16)
    s_gu = [p.sb(f"gu{o}", [128, W], BF16) for o in range(16)]
    s_row = p.sb("row", [1, W])

    P_a = [p.ps("Pa0", [128, W]), p.ps("Pa1", [128, W])]
    P_b = [p.ps("Pb0", [128, W]), p.ps("Pb1", [128, W])]
    P_st = p.ps("Pst", [128, W])
    P_bc = p.ps("Pbc", [128, W])

    for tt in range(NT):
        c0 = tt * W
        for i in range(8):
            dma(p, "gpsimd", s_ybf[i][:], y[i * 128:(i + 1) * 128, c0:c0 + W], writes=[f"ybf{i}"])
            dma(p, "sync", s_xln[i][:], xln[i * 128:(i + 1) * 128, c0:c0 + W], writes=[f"xl{i}"])
            dma(p, "sync", s_x[i][:], xT[i * 128:(i + 1) * 128, c0:c0 + W], writes=[f"rx{i}"])
        for m in range(8):
            pp, ppn = P_a[m % 2], f"Pa{m % 2}"
            for k in range(8):
                mm(p, pp[:], s_opT[k][:, m * 128:(m + 1) * 128], s_ybf[k][:], k == 0, k == 7,
                   reads=[f"opT{k}", f"ybf{k}"], writes=[ppn])
            vop(p, lambda e, m=m, pp=pp: e.scalar_tensor_tensor(s_x1[m][:], pp[:], s_prm[:, m:m + 1], s_xln[m][:], ALU.add, ALU.add),
                reads=[ppn, "prm", f"xl{m}"], writes=[f"x1_{m}"])
            vop(p, lambda e, m=m: e.tensor_add(s_x[m][:], s_x1[m][:], s_x[m][:]),
                reads=[f"x1_{m}", f"rx{m}"], writes=[f"rx{m}"])
        for m in range(8):
            act(p, s_ybf[m][:], s_x1[m][:], AF.Square, reads=[f"x1_{m}"], writes=[f"ybf{m}"])
        for i in range(8):
            mm(p, P_st[0:1, 0:W], s_onesb[:], s_ybf[i][:], i == 0, i == 7,
               reads=["onesb", f"ybf{i}"], writes=["Pst"])
        vop(p, lambda e: e.tensor_scalar_mul(s_row[:], P_st[0:1, 0:W], 1.0 / D), reads=["Pst"], writes=["row"])
        act(p, s_row[:], s_row[:], AF.Sqrt, bias=s_cst[0:1, 1:2], reads=["row", "cst"], writes=["row"])
        vop(p, lambda e: e.reciprocal(s_row[:], s_row[:]), reads=["row"], writes=["row"])
        mm(p, P_bc[0:128, 0:W], s_ones1[:], s_row[0:1, 0:W], True, True,
           reads=["ones1", "row"], writes=["Pbc"])
        for m in range(8):
            vop(p, lambda e, m=m: e.scalar_tensor_tensor(s_h2[m][:], s_x1[m][:], s_prm[:, 8 + m:9 + m], P_bc[:], ALU.mult, ALU.mult),
                reads=[f"x1_{m}", "prm", "Pbc"], writes=[f"h2_{m}"])
        for mo in range(16):
            pg, pgn = P_a[mo % 2], f"Pa{mo % 2}"
            pu, pun = P_b[mo % 2], f"Pb{mo % 2}"
            for k in range(8):
                mm(p, pg[:], s_gT[k][:, mo * 128:(mo + 1) * 128], s_h2[k][:], k == 0, k == 7,
                   reads=[f"gT{k}", f"h2_{k}"], writes=[pgn])
            for k in range(8):
                mm(p, pu[:], s_uT[k][:, mo * 128:(mo + 1) * 128], s_h2[k][:], k == 0, k == 7,
                   reads=[f"uT{k}", f"h2_{k}"], writes=[pun])
            act(p, s_sact[:], pg[:], AF.Silu, reads=[pgn], writes=["sact"])
            vop(p, lambda e, mo=mo, pu=pu: e.tensor_mul(s_gu[mo][:], s_sact[:], pu[:]),
                reads=["sact", pun], writes=[f"gu{mo}"])
        for m in range(8):
            pp, ppn = P_a[m % 2], f"Pa{m % 2}"
            for ko in range(16):
                mm(p, pp[:], s_dT[ko][:, m * 128:(m + 1) * 128], s_gu[ko][:], ko == 0, ko == 15,
                   reads=[f"dT{ko}", f"gu{ko}"], writes=[ppn])
            vop(p, lambda e, m=m, pp=pp: e.scalar_tensor_tensor(s_x1[m][:], pp[:], 1.0, s_x[m][:], ALU.mult, ALU.add),
                reads=[ppn, f"rx{m}", f"x1_{m}"], writes=[f"x1_{m}"]),
            dma(p, "gpsimd", o_x1[m * 128:(m + 1) * 128, c0:c0 + W], s_x1[m][:], reads=[f"x1_{m}"])
    return p


def _prep_s3_maps(inputs, xT_full, xln_full, y_full):
    opT = np.ascontiguousarray(np.asarray(inputs["L0_attn_out_proj_w"], np.float32).T)
    gT = np.ascontiguousarray(np.asarray(inputs["L0_attn_gate_w"], np.float32).T)
    uT = np.ascontiguousarray(np.asarray(inputs["L0_attn_up_w"], np.float32).T)
    dT = np.ascontiguousarray(np.asarray(inputs["L0_attn_down_w"], np.float32).T)
    prm = np.zeros((128, 16), np.float32)
    prm[:, 0:8] = np.asarray(inputs["L0_attn_out_proj_b"], np.float32).reshape(8, 128).T
    prm[:, 8:16] = np.asarray(inputs["L0_attn_norm2_w"], np.float32).reshape(8, 128).T
    cstv = np.zeros((128, 8), np.float32)
    cstv[:, 0] = 1e-5
    cstv[:, 1] = 1e-6
    cstv[:, 2] = 1.0
    cstv[:, 3] = -1.0
    maps = []
    for c in range(NCORES):
        sl = slice(c * TC, (c + 1) * TC)
        maps.append({
            "xT": np.ascontiguousarray(xT_full[:, sl]),
            "xln": np.ascontiguousarray(xln_full[:, sl]),
            "y": np.ascontiguousarray(y_full[:, sl]),
            "opT": opT, "gT": gT, "uT": uT, "dT": dT,
            "prm": prm, "cst": cstv,
        })
    return maps


def run_s3(inputs, xT_full, xln_full, y_full):
    p = build_s3()
    nc = p.finish()
    maps = _prep_s3_maps(inputs, xT_full, xln_full, y_full)
    res = run_bass_kernel_spmd(nc, maps, core_ids=list(range(NCORES)))
    return np.concatenate([r["x1"] for r in res.results], axis=1)


def build_s4():
    p = Prog()
    xT = p.dram("xT", [D, TC])
    w1T = p.dram("w1T", [D, 4096])
    w1r = p.dram("w1r", [4096, D])
    w2T = p.dram("w2T", [4096, D])
    w2r = p.dram("w2r", [D, 4096])
    prm = p.dram("prm", [128, 48])
    cst = p.dram("cst", [128, 8])
    o_x = p.dram("xo", [D, TC], kind="ExternalOutput")

    s_w1 = [p.sb(f"sw1_{k}", [128, 4096], BF16) for k in range(8)]
    s_w2 = [p.sb(f"sw2_{k}", [128, D], BF16) for k in range(32)]
    s_a1 = p.sb("a1", [128, 32])
    s_a2 = p.sb("a2", [128, 8])
    s_scr = p.sb("scr", [128, 2048])
    s_a2t = p.sb("a2t", [128, 8])
    s_prm = p.sb("c_prm", [128, 48])
    s_cst = p.sb("c_cst", [128, 8])
    s_ones1 = p.sb("ones1", [1, 128])
    s_onesb = p.sb("onesb", [128, 1], BF16)
    dma(p, "sync", s_prm[:], prm, writes=["prm"])
    dma(p, "sync", s_cst[:], cst, writes=["cst"])
    vop(p, lambda e: e.memset(s_ones1[:], 1.0), writes=["ones1"])
    vop(p, lambda e: e.memset(s_onesb[:], 1.0), writes=["onesb"])

    for k in range(8):
        for hlf in range(2):
            dma(p, "sync", s_scr[:], w1T[k * 128:(k + 1) * 128, hlf * 2048:(hlf + 1) * 2048], writes=["scr"])
            act(p, s_w1[k][:, hlf * 2048:(hlf + 1) * 2048], s_scr[:], AF.Sign, reads=["scr"], writes=[f"sw1_{k}"])
    for k in range(32):
        dma(p, "sync", s_scr[:, 0:D], w1r[k * 128:(k + 1) * 128, :], writes=["scr"])
        vop(p, lambda e, k=k: e.tensor_reduce(s_a1[:, k:k + 1], s_scr[:, 0:D], AX.X, ALU.add, apply_absolute_value=True),
            reads=["scr"], writes=["a1"])
        dma(p, "sync", s_scr[:, 0:D], w2T[k * 128:(k + 1) * 128, :], writes=["scr"])
        act(p, s_w2[k][:], s_scr[:, 0:D], AF.Sign, reads=["scr"], writes=[f"sw2_{k}"])
    vop(p, lambda e: e.tensor_scalar_mul(s_a1[:], s_a1[:], 1.0 / D), reads=["a1"], writes=["a1"])
    for k in range(8):
        dma(p, "sync", s_scr[:], w2r[k * 128:(k + 1) * 128, 0:2048], writes=["scr"])
        vop(p, lambda e, k=k: e.tensor_reduce(s_a2[:, k:k + 1], s_scr[:], AX.X, ALU.add, apply_absolute_value=True),
            reads=["scr"], writes=["a2"])
        dma(p, "sync", s_scr[:], w2r[k * 128:(k + 1) * 128, 2048:4096], writes=["scr"])
        vop(p, lambda e, k=k: e.tensor_reduce(s_a2t[:, k:k + 1], s_scr[:], AX.X, ALU.add, apply_absolute_value=True),
            reads=["scr"], writes=["a2t"])
    vop(p, lambda e: e.tensor_add(s_a2[:], s_a2[:], s_a2t[:]), reads=["a2", "a2t"], writes=["a2"])
    vop(p, lambda e: e.tensor_scalar_mul(s_a2[:], s_a2[:], 1.0 / 4096.0), reads=["a2"], writes=["a2"])

    s_x = [p.sb(f"rx{i}", [128, W]) for i in range(8)]
    s_xbf = [p.sb(f"xbf{i}", [128, W], BF16) for i in range(8)]
    s_t1 = p.sb("t1s", [128, W])
    s_h = [p.sb(f"hb{o}", [128, W], BF16) for o in range(32)]
    s_mu = p.sb("mu", [1, W])
    s_var = p.sb("var", [1, W])
    s_rstd = p.sb("rstd", [1, W])
    s_out = p.sb("outs", [128, W])
    P_stx = p.ps("Pstx", [128, W])
    P_stq = p.ps("Pstq", [128, W])
    P_mu = p.ps("Pmu", [128, W])
    P_rs = p.ps("Prs", [128, W])
    P_a = [p.ps("Pa0", [128, W]), p.ps("Pa1", [128, W])]

    for tt in range(NT):
        c0 = tt * W
        for i in range(8):
            dma(p, "sync", s_x[i][:], xT[i * 128:(i + 1) * 128, c0:c0 + W], writes=[f"rx{i}"])
            act(p, s_h[i][:], s_x[i][:], AF.Square, reads=[f"rx{i}"], writes=[f"hb{i}"])
            vop(p, lambda e, i=i: e.tensor_copy(s_xbf[i][:], s_x[i][:]), reads=[f"rx{i}"], writes=[f"xbf{i}"])
        for i in range(8):
            mm(p, P_stx[0:1, 0:W], s_onesb[:], s_xbf[i][:], i == 0, i == 7,
               reads=["onesb", f"xbf{i}"], writes=["Pstx"])
        for i in range(8):
            mm(p, P_stq[0:1, 0:W], s_onesb[:], s_h[i][:], i == 0, i == 7,
               reads=["onesb", f"hb{i}"], writes=["Pstq"])
        vop(p, lambda e: e.tensor_scalar_mul(s_mu[:], P_stx[0:1, 0:W], 1.0 / D), reads=["Pstx"], writes=["mu"])
        vop(p, lambda e: e.tensor_scalar_mul(s_rstd[:], P_stq[0:1, 0:W], 1.0 / D), reads=["Pstq"], writes=["rstd"])
        vop(p, lambda e: e.tensor_mul(s_var[:], s_mu[:], s_mu[:]), reads=["mu"], writes=["var"])
        vop(p, lambda e: e.tensor_sub(s_var[:], s_rstd[:], s_var[:]), reads=["rstd", "var"], writes=["var"])
        act(p, s_rstd[:], s_var[:], AF.Sqrt, bias=s_cst[0:1, 0:1], reads=["var", "cst"], writes=["rstd"])
        vop(p, lambda e: e.reciprocal(s_rstd[:], s_rstd[:]), reads=["rstd"], writes=["rstd"])
        mm(p, P_mu[0:128, 0:W], s_ones1[:], s_mu[0:1, 0:W], True, True, reads=["ones1", "mu"], writes=["Pmu"])
        mm(p, P_rs[0:128, 0:W], s_ones1[:], s_rstd[0:1, 0:W], True, True, reads=["ones1", "rstd"], writes=["Prs"])
        for i in range(8):
            vop(p, lambda e, i=i: e.scalar_tensor_tensor(s_t1[:], P_mu[:], -1.0, s_x[i][:], ALU.mult, ALU.add),
                reads=["Pmu", f"rx{i}"], writes=["t1s"])
            vop(p, lambda e, i=i: e.tensor_mul(s_xbf[i][:], s_t1[:], P_rs[:]),
                reads=["t1s", "Prs"], writes=[f"xbf{i}"])
        for mo in range(32):
            pp, ppn = P_a[mo % 2], f"Pa{mo % 2}"
            for k in range(8):
                mm(p, pp[:], s_w1[k][:, mo * 128:(mo + 1) * 128], s_xbf[k][:], k == 0, k == 7,
                   reads=[f"sw1_{k}", f"xbf{k}"], writes=[ppn])
            act(p, s_h[mo][:], pp[:], AF.Gelu, bias=s_prm[:, mo:mo + 1], scale=s_a1[:, mo:mo + 1],
                reads=[ppn, "prm", "a1"], writes=[f"hb{mo}"])
        for m in range(8):
            pp, ppn = P_a[m % 2], f"Pa{m % 2}"
            for ko in range(32):
                mm(p, pp[:], s_w2[ko][:, m * 128:(m + 1) * 128], s_h[ko][:], ko == 0, ko == 31,
                   reads=[f"sw2_{ko}", f"hb{ko}"], writes=[ppn])
            vop(p, lambda e, m=m, pp=pp: e.tensor_scalar(s_out[:], pp[:], s_a2[:, m:m + 1], s_prm[:, 32 + m:33 + m], ALU.mult, ALU.add),
                reads=[ppn, "a2", "prm"], writes=["outs"])
            vop(p, lambda e, m=m: e.tensor_add(s_out[:], s_out[:], s_x[m][:]),
                reads=["outs", f"rx{m}"], writes=["outs"])
            dma(p, "gpsimd", o_x[m * 128:(m + 1) * 128, c0:c0 + W], s_out[:], reads=["outs"])
    return p


def _prep_s4_maps(inputs, xT_full):
    w1 = np.asarray(inputs["L0_ffn_w1"], np.float32)
    w2 = np.asarray(inputs["L0_ffn_w2"], np.float32)
    prm = np.zeros((128, 48), np.float32)
    prm[:, 0:32] = np.asarray(inputs["L0_ffn_b1"], np.float32).reshape(32, 128).T
    prm[:, 32:40] = np.asarray(inputs["L0_ffn_b2"], np.float32).reshape(8, 128).T
    cstv = np.zeros((128, 8), np.float32)
    cstv[:, 0] = 1e-5
    cstv[:, 1] = 1e-6
    w1T = np.ascontiguousarray(w1.T)
    w2T = np.ascontiguousarray(w2.T)
    maps = []
    for c in range(NCORES):
        maps.append({
            "xT": np.ascontiguousarray(xT_full[:, c * TC:(c + 1) * TC]),
            "w1T": w1T, "w1r": w1, "w2T": w2T, "w2r": w2,
            "prm": prm, "cst": cstv,
        })
    return maps


def run_s4(inputs, xT_full):
    p = build_s4()
    nc = p.finish()
    maps = _prep_s4_maps(inputs, xT_full)
    res = run_bass_kernel_spmd(nc, maps, core_ids=list(range(NCORES)))
    return np.concatenate([r["xo"] for r in res.results], axis=1)


def build_s5():
    p = Prog()
    xT = p.dram("xT", [D, TC])
    WvT = p.dram("WvT", [D, D])
    WqT = p.dram("WqT", [D, D])
    WkT = p.dram("WkT", [D, D])
    WoT = p.dram("WoT", [D, D])
    prm = p.dram("prm", [128, 32])
    beta = p.dram("beta", [1, 16])
    blk1 = p.dram("blk1", [128, 2])
    blk2 = p.dram("blk2", [2, 128])
    cst = p.dram("cst", [128, 8])
    scr = p.dram("scr16", [1, 16], kind="Internal")
    o_x = p.dram("xo", [D, TC], kind="ExternalOutput")

    s_Wv = [p.sb(f"Wv{k}", [128, D], BF16) for k in range(8)]
    s_Wq = [p.sb(f"Wq{k}", [128, D], BF16) for k in range(8)]
    s_Wk = [p.sb(f"Wk{k}", [128, D], BF16) for k in range(8)]
    s_Wo = [p.sb(f"Wo{k}", [128, D], BF16) for k in range(8)]
    for k in range(8):
        dma(p, "gpsimd", s_Wv[k][:], WvT[k * 128:(k + 1) * 128, :], writes=[f"Wv{k}"], group="wl")
        dma(p, "gpsimd", s_Wq[k][:], WqT[k * 128:(k + 1) * 128, :], writes=[f"Wq{k}"], group="wl")
        dma(p, "gpsimd", s_Wk[k][:], WkT[k * 128:(k + 1) * 128, :], writes=[f"Wk{k}"], group="wl")
        dma(p, "gpsimd", s_Wo[k][:], WoT[k * 128:(k + 1) * 128, :], writes=[f"Wo{k}"], group="wl")
    p.fw.settle("wl", [f"W{t}{k}" for t in "vqko" for k in range(8)])
    s_prm = p.sb("c_prm", [128, 32])
    s_cst = p.sb("c_cst", [128, 8])
    s_blk1 = p.sb("c_blk1", [128, 2], BF16)
    s_blk2 = p.sb("c_blk2", [2, 128])
    s_beta = p.sb("c_beta", [1, 16])
    s_scl = p.sb("scl", [2, 8])
    s_ones1 = p.sb("ones1", [1, 128])
    s_onesb = p.sb("onesb", [128, 1], BF16)
    dma(p, "sync", s_prm[:], prm, writes=["prm"])
    dma(p, "sync", s_cst[:], cst, writes=["cst"])
    dma(p, "gpsimd", s_blk1[:], blk1, writes=["blk1"])
    dma(p, "sync", s_blk2[:], blk2, writes=["blk2"])
    dma(p, "sync", s_beta[:], beta, writes=["beta"])
    vop(p, lambda e: e.memset(s_ones1[:], 1.0), writes=["ones1"])
    vop(p, lambda e: e.memset(s_onesb[:], 1.0), writes=["onesb"])
    act(p, s_beta[:], s_beta[:], AF.Exp, reads=["beta"], writes=["beta"])
    vop(p, lambda e: e.tensor_scalar_mul(s_beta[:], s_beta[:], 8.0), reads=["beta"], writes=["beta"])
    vop(p, lambda e: e.reciprocal(s_beta[:], s_beta[:]), reads=["beta"], writes=["beta"])
    dma(p, "sync", scr, s_beta[:], reads=["beta"], writes=["scrd"])
    dma(p, "sync", s_scl[:], scr.rearrange("a (c r) -> r (a c)", r=2), reads=["scrd"], writes=["scl"], allow_slow_non_contiguous=True)

    s_x = [p.sb(f"rx{i}", [128, W]) for i in range(8)]
    s_xbf = [p.sb(f"xbf{i}", [128, W], BF16) for i in range(8)]
    s_sq = [p.sb(f"sq{i}", [128, W], BF16) for i in range(8)]
    s_v = [p.sb(f"vv{i}", [128, W]) for i in range(8)]
    s_qs = p.sb("qs", [128, W])
    s_qk = p.sb("qk", [128, W], BF16)
    s_ctx = [p.sb(f"ctx{i}", [128, W], BF16) for i in range(8)]
    s_sc = p.sb("sc", [2, 8 * W])
    s_e = p.sb("se", [2, 8 * W])
    s_at = p.sb("sat", [2, 8 * W])
    s_t1 = p.sb("t1s", [128, W])
    s_mu = p.sb("mu", [1, W])
    s_var = p.sb("var", [1, W])
    s_rstd = p.sb("rstd", [1, W])
    s_out = p.sb("outs", [128, W])
    P_stx = p.ps("Pstx", [128, W])
    P_stq = p.ps("Pstq", [128, W])
    P_mu = p.ps("Pmu", [128, W])
    P_rs = p.ps("Prs", [128, W])
    P_v = p.ps("Pv", [128, W])
    P_q = p.ps("Pq", [128, W])
    P_k = p.ps("Pk", [128, W])
    P_sc = p.ps("Psc", [128, W])

    for tt in range(NT):
        c0 = tt * W
        for i in range(8):
            dma(p, "sync", s_x[i][:], xT[i * 128:(i + 1) * 128, c0:c0 + W], writes=[f"rx{i}"])
            act(p, s_sq[i][:], s_x[i][:], AF.Square, reads=[f"rx{i}"], writes=[f"sq{i}"])
            vop(p, lambda e, i=i: e.tensor_copy(s_xbf[i][:], s_x[i][:]), reads=[f"rx{i}"], writes=[f"xbf{i}"])
        for i in range(8):
            mm(p, P_stx[0:1, 0:W], s_onesb[:], s_xbf[i][:], i == 0, i == 7,
               reads=["onesb", f"xbf{i}"], writes=["Pstx"])
        for i in range(8):
            mm(p, P_stq[0:1, 0:W], s_onesb[:], s_sq[i][:], i == 0, i == 7,
               reads=["onesb", f"sq{i}"], writes=["Pstq"])
        vop(p, lambda e: e.tensor_scalar_mul(s_mu[:], P_stx[0:1, 0:W], 1.0 / D), reads=["Pstx"], writes=["mu"])
        vop(p, lambda e: e.tensor_scalar_mul(s_rstd[:], P_stq[0:1, 0:W], 1.0 / D), reads=["Pstq"], writes=["rstd"])
        vop(p, lambda e: e.tensor_mul(s_var[:], s_mu[:], s_mu[:]), reads=["mu"], writes=["var"])
        vop(p, lambda e: e.tensor_sub(s_var[:], s_rstd[:], s_var[:]), reads=["rstd", "var"], writes=["var"])
        act(p, s_rstd[:], s_var[:], AF.Sqrt, bias=s_cst[0:1, 0:1], reads=["var", "cst"], writes=["rstd"])
        vop(p, lambda e: e.reciprocal(s_rstd[:], s_rstd[:]), reads=["rstd"], writes=["rstd"])
        mm(p, P_mu[0:128, 0:W], s_ones1[:], s_mu[0:1, 0:W], True, True, reads=["ones1", "mu"], writes=["Pmu"])
        mm(p, P_rs[0:128, 0:W], s_ones1[:], s_rstd[0:1, 0:W], True, True, reads=["ones1", "rstd"], writes=["Prs"])
        for i in range(8):
            vop(p, lambda e, i=i: e.scalar_tensor_tensor(s_t1[:], P_mu[:], -1.0, s_x[i][:], ALU.mult, ALU.add),
                reads=["Pmu", f"rx{i}"], writes=["t1s"])
            vop(p, lambda e, i=i: e.tensor_mul(s_xbf[i][:], s_t1[:], P_rs[:]),
                reads=["t1s", "Prs"], writes=[f"xbf{i}"])
        for m in range(8):
            for k in range(8):
                mm(p, P_v[:], s_Wv[k][:, m * 128:(m + 1) * 128], s_xbf[k][:], k == 0, k == 7,
                   reads=[f"Wv{k}", f"xbf{k}"], writes=["Pv"])
            vop(p, lambda e, m=m: e.tensor_scalar(s_v[m][:], P_v[:], s_prm[:, m:m + 1], None, ALU.add),
                reads=["Pv", "prm"], writes=[f"vv{m}"])
            for k in range(8):
                mm(p, P_q[:], s_Wq[k][:, m * 128:(m + 1) * 128], s_xbf[k][:], k == 0, k == 7,
                   reads=[f"Wq{k}", f"xbf{k}"], writes=["Pq"])
            vop(p, lambda e, m=m: e.tensor_scalar(s_qs[:], P_q[:], s_prm[:, 8 + m:9 + m], None, ALU.add),
                reads=["Pq", "prm"], writes=["qs"])
            for k in range(8):
                mm(p, P_k[:], s_Wk[k][:, m * 128:(m + 1) * 128], s_xbf[k][:], k == 0, k == 7,
                   reads=[f"Wk{k}", f"xbf{k}"], writes=["Pk"])
            vop(p, lambda e, m=m: e.scalar_tensor_tensor(s_qk[:], P_k[:], s_prm[:, 16 + m:17 + m], s_qs[:], ALU.add, ALU.mult),
                reads=["Pk", "prm", "qs"], writes=["qk"])
            mm(p, P_sc[0:2, 0:W], s_blk1[:], s_qk[:], True, True, reads=["blk1", "qk"], writes=["Psc"])
            vop(p, lambda e, m=m: e.tensor_scalar(s_sc[0:2, m * W:(m + 1) * W], P_sc[0:2, 0:W], s_scl[0:2, m:m + 1], None, ALU.mult),
                reads=["Psc", "scl"], writes=["ssc"])
        vop(p, lambda e: e.tensor_sub(s_e[:], s_sc[:], s_sc[:]), reads=["ssc"], writes=["sse"])
        act(p, s_e[:], s_e[:], AF.Exp, reads=["sse"], writes=["sse"])
        vop(p, lambda e: e.reciprocal(s_at[:], s_e[:]), reads=["sse"], writes=["sat"])
        vop(p, lambda e: e.tensor_mul(s_at[:], s_at[:], s_e[:]), reads=["sat", "sse"], writes=["sat"])
        for m in range(8):
            mm(p, P_v[0:128, 0:W], s_blk2[:], s_at[0:2, m * W:(m + 1) * W], True, True,
               reads=["blk2", "sat"], writes=["Pv"])
            vop(p, lambda e, m=m: e.tensor_mul(s_ctx[m][:], s_v[m][:], P_v[:]),
                reads=[f"vv{m}", "Pv"], writes=[f"ctx{m}"])
        for m in range(8):
            for k in range(8):
                mm(p, P_q[:], s_Wo[k][:, m * 128:(m + 1) * 128], s_ctx[k][:], k == 0, k == 7,
                   reads=[f"Wo{k}", f"ctx{k}"], writes=["Pq"])
            vop(p, lambda e, m=m: e.scalar_tensor_tensor(s_out[:], P_q[:], s_prm[:, 24 + m:25 + m], s_x[m][:], ALU.add, ALU.add),
                reads=["Pq", "prm", f"rx{m}"], writes=["outs"])
            dma(p, "gpsimd", o_x[m * 128:(m + 1) * 128, c0:c0 + W], s_out[:], reads=["outs"])
    return p


def _prep_s5_maps(inputs, xT_full):
    prm = np.zeros((128, 32), np.float32)
    prm[:, 0:8] = np.asarray(inputs["L1_attn_bv"], np.float32).reshape(8, 128).T
    prm[:, 8:16] = np.asarray(inputs["L1_attn_bq"], np.float32).reshape(8, 128).T
    prm[:, 16:24] = np.asarray(inputs["L1_attn_bk"], np.float32).reshape(8, 128).T
    prm[:, 24:32] = np.asarray(inputs["L1_attn_bo"], np.float32).reshape(8, 128).T
    cstv = np.zeros((128, 8), np.float32)
    cstv[:, 0] = 1e-5
    cstv[:, 1] = 1e-6
    blk1 = np.zeros((128, 2), np.float32)
    blk1[0:64, 0] = 1.0
    blk1[64:128, 1] = 1.0
    blk2 = np.zeros((2, 128), np.float32)
    blk2[0, 0:64] = 1.0
    blk2[1, 64:128] = 1.0
    maps = []
    for c in range(NCORES):
        maps.append({
            "xT": np.ascontiguousarray(xT_full[:, c * TC:(c + 1) * TC]),
            "WvT": np.ascontiguousarray(np.asarray(inputs["L1_attn_Wv"], np.float32).T),
            "WqT": np.ascontiguousarray(np.asarray(inputs["L1_attn_Wq"], np.float32).T),
            "WkT": np.ascontiguousarray(np.asarray(inputs["L1_attn_Wk"], np.float32).T),
            "WoT": np.ascontiguousarray(np.asarray(inputs["L1_attn_Wo"], np.float32).T),
            "prm": prm, "beta": np.asarray(inputs["L1_attn_beta"], np.float32).reshape(1, 16),
            "blk1": blk1, "blk2": blk2, "cst": cstv,
        })
    return maps


def run_s5(inputs, xT_full):
    p = build_s5()
    nc = p.finish()
    maps = _prep_s5_maps(inputs, xT_full)
    res = run_bass_kernel_spmd(nc, maps, core_ids=list(range(NCORES)))
    return np.concatenate([r["xo"] for r in res.results], axis=1)


TH = 2560

def build_s6(with_ln):
    p = Prog()
    f_in = p.dram("f", [D, TH + 1])
    wqkv = p.dram("wqkv", [D, 1536])
    wo = p.dram("wo", [512, D])
    qsr = p.dram("qsr", [1, 64])
    ksr = p.dram("ksr", [1, 64])
    cq8 = p.dram("cq8", [TH, 64])
    snq = p.dram("snq", [TH, 64])
    ck = p.dram("ck", [TH, 64])
    snk = p.dram("snk", [TH, 64])
    amask = p.dram("amask", [4, 128, 4096], BF16)
    iden = p.dram("iden", [128, 128])
    cst = p.dram("cst", [128, 8])
    o_f = p.dram("fo", [D, TC], kind="ExternalOutput")
    if with_ln:
        o_z = p.dram("zo", [D, TC], kind="ExternalOutput")

    s_wq = [p.sb(f"wq{k}", [128, 1536], BF16) for k in range(8)]
    s_wo = [p.sb(f"swo{r}", [128, D], BF16) for r in range(4)]
    for k in range(8):
        dma(p, "gpsimd", s_wq[k][:], wqkv[k * 128:(k + 1) * 128, :], writes=[f"wq{k}"], group="wl")
    for r in range(4):
        dma(p, "gpsimd", s_wo[r][:], wo[r * 128:(r + 1) * 128, :], writes=[f"swo{r}"], group="wl")
    p.fw.settle("wl", [f"wq{k}" for k in range(8)] + [f"swo{r}" for r in range(4)])
    s_iden = p.sb("c_iden", [128, 128], BF16)
    dma(p, "gpsimd", s_iden[:], iden, writes=["iden"])
    s_cst = p.sb("c_cst", [128, 8])
    dma(p, "sync", s_cst[:], cst, writes=["cst"])
    s_ones1 = p.sb("ones1", [1, 128])
    s_onesb = p.sb("onesb", [128, 1], BF16)
    vop(p, lambda e: e.memset(s_ones1[:], 1.0), writes=["ones1"])
    vop(p, lambda e: e.memset(s_onesb[:], 1.0), writes=["onesb"])
    s_qsrow = p.sb("qsrow", [1, 64])
    s_ksrow = p.sb("ksrow", [1, 64])
    dma(p, "sync", s_qsrow[:], qsr, writes=["qsrow"])
    dma(p, "sync", s_ksrow[:], ksr, writes=["ksrow"])
    s_qsb = p.sb("qsb", [128, 64])
    s_ksb = p.sb("ksb", [128, 64])
    s_msk = p.sb("mskb", [128, 4 * 1024], BF16)

    s_x = [p.sb(f"rx{i}", [128, 513]) for i in range(8)]
    s_xbf = [p.sb(f"xbf{i}", [128, 513], BF16) for i in range(8)]
    if with_ln:
        s_sq = [p.sb(f"sq{i}", [128, 513], BF16) for i in range(8)]
        s_z = s_x
    s_qkv = [p.sb(f"qkv{mo}", [128, W], BF16) for mo in range(12)]
    s_vtm = [p.sb(f"vtm{jc}", [128, 512], BF16) for jc in range(20)]
    s_k2 = p.sb("k2b", [128, 20 * 512], BF16)
    s_q2 = p.sb("q2b", [128, 16 * 512], BF16)
    s_tm = [p.sb(f"tm{j}", [128, 512]) for j in range(2)]
    s_ss = [p.sb(f"ss{j}", [128, 8]) for j in range(2)]
    s_t1 = [p.sb(f"t1b{j}", [128, 512]) for j in range(2)]
    s_t2 = [p.sb(f"t2b{j}", [128, 512]) for j in range(2)]
    s_tbl = [p.sb(f"tbl{j}", [128, 128]) for j in range(2)]
    s_tmbf = [p.sb(f"tmbf{j}", [128, 512], BF16) for j in range(2)]
    s_scs = [p.sb(f"scs{j}", [128, 1024]) for j in range(2)]
    s_nmx = [p.sb(f"nmx{j}", [128, 1]) for j in range(2)]
    s_abf = [p.sb(f"abf{j}", [128, 1024], BF16) for j in range(2)]
    s_rs = [p.sb(f"rsum{j}", [128, 1]) for j in range(2)]
    s_aT = [p.sb(f"aTs{j}", [128, 1024], BF16) for j in range(2)]
    s_otm = p.sb("otm", [128, 512], BF16)
    s_ofm = p.sb("ofmb", [128, 16 * 512], BF16)
    s_mu = p.sb("mu", [1, 513])
    s_var = p.sb("var", [1, 513])
    s_rstd = p.sb("rstd", [1, 513])

    P_A = p.ps("PA", [128, 1024])
    P_B = p.ps("PB", [128, 1024])
    _pc = p.ps("PC0", [128, W])
    P_C = [_pc, _pc]
    P_T = [p.ps("PT0", [128, 1024], BF16), p.ps("PT1", [128, 1024], BF16)]
    _px = p.ps("PX0", [128, 64])
    P_X = [_px, _px]

    mm(p, P_X[0][0:128, 0:64], s_ones1[:], s_qsrow[0:1, 0:64], True, True, reads=["ones1", "qsrow"], writes=["PX0"])
    act(p, s_qsb[:], P_X[0][:], AF.Copy, reads=["PX0"], writes=["qsb"])
    mm(p, P_X[1][0:128, 0:64], s_ones1[:], s_ksrow[0:1, 0:64], True, True, reads=["ones1", "ksrow"], writes=["PX0"])
    act(p, s_ksb[:], P_X[1][:], AF.Copy, reads=["PX0"], writes=["ksb"])

    def PTh(par):
        return P_T[par][:, 0:512], f"PT{par}"

    def h3(buf):
        return buf[:].rearrange("p (h d) -> p h d", h=8)

    def tmchunk_rotary(jc, isq, par):
        ct, st = (cq8, snq) if isq else (ck, snk)
        scb, scn = (s_qsb, "qsb") if isq else (s_ksb, "ksb")
        tm, tmn = s_tm[par], f"tm{par}"
        t1, t1n = s_t1[par], f"t1b{par}"
        t2, t2n = s_t2[par], f"t2b{par}"
        ss, ssn = s_ss[par], f"ss{par}"
        tb, tbn = s_tbl[par], f"tbl{par}"
        dma(p, "sync", tb[:, 0:64], ct[jc * 128:(jc + 1) * 128, :], writes=[tbn], group=tbn)
        dma(p, "sync", tb[:, 64:128], st[jc * 128:(jc + 1) * 128, :], writes=[tbn], group=tbn)
        vop(p, lambda e: e.tensor_mul(t2[:], tm[:], tm[:]), reads=[tmn], writes=[t2n])
        vop(p, lambda e: e.tensor_reduce(ss[:].unsqueeze(2), h3(t2), AX.X, ALU.add),
            reads=[t2n], writes=[ssn])
        act(p, ss[:], ss[:], AF.Sqrt, reads=[ssn], writes=[ssn])
        vop(p, lambda e: e.tensor_scalar_max(ss[:], ss[:], 1e-12), reads=[ssn], writes=[ssn])
        vop(p, lambda e: e.reciprocal(ss[:], ss[:]), reads=[ssn], writes=[ssn])
        vop(p, lambda e: e.tensor_tensor(h3(t1), h3(tm), ss[:].unsqueeze(2).broadcast_to([128, 8, 64]), ALU.mult),
            reads=[tmn, ssn], writes=[t1n])
        vop(p, lambda e: e.tensor_tensor(h3(t1), h3(t1), scb[:].unsqueeze(1).broadcast_to([128, 8, 64]), ALU.mult),
            reads=[t1n, scn], writes=[t1n])
        vop(p, lambda e: e.tensor_tensor(h3(t2)[:, :, 0:32], h3(t1)[:, :, 32:64],
                                         tb[:, 64:96].unsqueeze(1).broadcast_to([128, 8, 32]), ALU.mult),
            reads=[t1n, tbn], writes=[t2n])
        vop(p, lambda e: e.tensor_tensor(h3(t2)[:, :, 32:64], h3(t1)[:, :, 0:32],
                                         tb[:, 96:128].unsqueeze(1).broadcast_to([128, 8, 32]), ALU.mult),
            reads=[t1n, tbn], writes=[t2n])
        vop(p, lambda e: e.tensor_tensor(h3(t1), h3(t1), tb[:, 0:64].unsqueeze(1).broadcast_to([128, 8, 64]), ALU.mult),
            reads=[t1n, tbn], writes=[t1n])
        vop(p, lambda e: e.tensor_add(t1[:], t1[:], t2[:]), reads=[t1n, t2n], writes=[t1n])
        vop(p, lambda e: e.tensor_copy(s_tmbf[par][:], t1[:]), reads=[t1n], writes=[f"tmbf{par}"])

    def batched_T(srcs, dst, dstname, par):
        pt, ptn = PTh(par)
        for r, (ap, rn) in enumerate(srcs):
            p.fw.op("tensor", lambda e, ap=ap, r=r, pt=pt: e.transpose(pt[:, r * 128:(r + 1) * 128], ap, s_iden[:]),
                    reads=[rn, "iden"], writes=[ptn])
        act(p, dst, pt[:], AF.Copy, reads=[ptn], writes=[dstname])

    for j in range(5):
        c0 = j * 512
        for i in range(8):
            dma(p, "sync", s_x[i][:], f_in[i * 128:(i + 1) * 128, c0:c0 + 513], writes=[f"rx{i}"])
        if with_ln:
            for i in range(8):
                act(p, s_sq[i][:], s_x[i][:], AF.Square, reads=[f"rx{i}"], writes=[f"sq{i}"])
                vop(p, lambda e, i=i: e.tensor_copy(s_xbf[i][:], s_x[i][:]), reads=[f"rx{i}"], writes=[f"xbf{i}"])
            for c, n in ((0, 512), (512, 1)):
                for i in range(8):
                    mm(p, P_A[0:1, c:c + n], s_onesb[:], s_xbf[i][:, c:c + n], i == 0, i == 7,
                       reads=["onesb", f"xbf{i}"], writes=["PA"])
                for i in range(8):
                    mm(p, P_B[0:1, c:c + n], s_onesb[:], s_sq[i][:, c:c + n], i == 0, i == 7,
                       reads=["onesb", f"sq{i}"], writes=["PB"])
            vop(p, lambda e: e.tensor_scalar_mul(s_mu[:], P_A[0:1, 0:513], 1.0 / D), reads=["PA"], writes=["mu"])
            vop(p, lambda e: e.tensor_scalar_mul(s_rstd[:], P_B[0:1, 0:513], 1.0 / D), reads=["PB"], writes=["rstd"])
            vop(p, lambda e: e.tensor_mul(s_var[:], s_mu[:], s_mu[:]), reads=["mu"], writes=["var"])
            vop(p, lambda e: e.tensor_sub(s_var[:], s_rstd[:], s_var[:]), reads=["rstd", "var"], writes=["var"])
            act(p, s_rstd[:], s_var[:], AF.Sqrt, bias=s_cst[0:1, 0:1], reads=["var", "cst"], writes=["rstd"])
            vop(p, lambda e: e.reciprocal(s_rstd[:], s_rstd[:]), reads=["rstd"], writes=["rstd"])
            for c, n in ((0, 512), (512, 1)):
                mm(p, P_B[0:128, c:c + n], s_ones1[:], s_mu[0:1, c:c + n], True, True,
                   reads=["ones1", "mu"], writes=["PB"])
            for c, n in ((0, 512), (512, 1)):
                mm(p, P_A[0:128, c:c + n], s_ones1[:], s_rstd[0:1, c:c + n], True, True,
                   reads=["ones1", "rstd"], writes=["PA"])
            for i in range(8):
                vop(p, lambda e, i=i: e.tensor_sub(s_z[i][:], s_x[i][:], P_B[:, 0:513]),
                    reads=[f"rx{i}", "PB"], writes=[f"rx{i}"])
                vop(p, lambda e, i=i: e.tensor_mul(s_z[i][:], s_z[i][:], P_A[:, 0:513]),
                    reads=[f"rx{i}", "PA"], writes=[f"rx{i}"])
                vop(p, lambda e, i=i: e.tensor_copy(s_xbf[i][:], s_z[i][:]), reads=[f"rx{i}"], writes=[f"xbf{i}"])
            if j >= 1:
                for i in range(8):
                    dma(p, "gpsimd", o_z[i * 128:(i + 1) * 128, c0 - 512:c0], s_z[i][:, 1:513],
                        reads=[f"rx{i}"], writes=["zodram"], group="zow")
        else:
            for i in range(8):
                vop(p, lambda e, i=i: e.tensor_copy(s_xbf[i][:], s_x[i][:]), reads=[f"rx{i}"], writes=[f"xbf{i}"])
        for mo in range(12):
            pp, ppn = P_C[mo % 2], "PC0"
            for k in range(8):
                view = s_xbf[k][:, 1:513] if k < 4 else s_xbf[k][:, 0:512]
                mm(p, pp[:], s_wq[k][:, mo * 128:(mo + 1) * 128], view, k == 0, k == 7,
                   reads=[f"wq{k}", f"xbf{k}"], writes=[ppn])
            act(p, s_qkv[mo][:], pp[:], AF.Copy, reads=[ppn], writes=[f"qkv{mo}"])
        for c in range(4):
            jc = j * 4 + c
            par = c % 2
            batched_T([(s_qkv[8 + r][:, c * 128:(c + 1) * 128], f"qkv{8 + r}") for r in range(4)],
                      s_vtm[jc][:], f"vtm{jc}", par)
            batched_T([(s_qkv[4 + r][:, c * 128:(c + 1) * 128], f"qkv{4 + r}") for r in range(4)],
                      s_tm[par][:], f"tm{par}", par)
            tmchunk_rotary(jc, False, par)
            batched_T([(s_tmbf[par][:, r * 128:(r + 1) * 128], f"tmbf{par}") for r in range(4)],
                      s_k2[:, jc * 512:(jc + 1) * 512], "k2b", par)
            if j >= 1:
                batched_T([(s_qkv[r][:, c * 128:(c + 1) * 128], f"qkv{r}") for r in range(4)],
                          s_tm[par][:], f"tm{par}", par)
                tmchunk_rotary(jc, True, par)
                oc = (jc - 4) * 512
                batched_T([(s_tmbf[par][:, r * 128:(r + 1) * 128], f"tmbf{par}") for r in range(4)],
                          s_q2[:, oc:oc + 512], "q2b", par)
    for w in range(4):
        dma(p, "gpsimd", s_msk[:], amask[w, :, :], writes=["mskb"])
        for qt in range(4):
            for h in range(8):
                r, hh = h // 2, (h % 2) * 64
                par = h % 2
                pp, ppn = (P_A, "PA") if par == 0 else (P_B, "PB")
                qv = s_q2[hh:hh + 64, (w + 1 + qt // 4) * 0 + ((w * 4 + qt) * 512 + r * 128):((w * 4 + qt) * 512 + r * 128) + 128]
                for half in range(2):
                    kv = s_k2[hh:hh + 64, :].rearrange("p (jc r t) -> p jc r t", r=4, t=128)[:, 4 * w + 4 * half:4 * w + 4 * half + 4, r, :]
                    mm(p, pp[:, half * 512:(half + 1) * 512], qv, kv,
                       True, True, reads=["q2b", "k2b"], writes=[ppn])
                vop(p, lambda e, pp=pp, qt=qt, par=par: e.tensor_add(s_scs[par][:], pp[:], s_msk[:, qt * 1024:(qt + 1) * 1024]),
                    reads=[ppn, "mskb"], writes=[f"scs{par}"])
                vop(p, lambda e, par=par: e.reduce_max(s_nmx[par][:, 0:1], s_scs[par][:], AX.X, negate=True),
                    reads=[f"scs{par}"], writes=[f"nmx{par}"])
                act(p, s_abf[par][:], s_scs[par][:], AF.Exp, bias=s_nmx[par][:, 0:1], accum_out=s_rs[par][:, 0:1],
                    reads=[f"scs{par}", f"nmx{par}"], writes=[f"abf{par}", f"rsum{par}"])
                vop(p, lambda e, par=par: e.reciprocal(s_rs[par][:], s_rs[par][:]),
                    reads=[f"rsum{par}"], writes=[f"rsum{par}"])
                for g in range(2):
                    batched_T([(s_abf[par][:, (4 * g + r2) * 128:(4 * g + r2) * 128 + 128], f"abf{par}") for r2 in range(4)],
                              s_aT[par][:, g * 512:(g + 1) * 512], f"aTs{par}", par)
                px, pxn = P_X[par], "PX0"
                for kc in range(8):
                    mm(p, px[:, 0:64], s_aT[par][:, kc * 128:(kc + 1) * 128], s_vtm[4 * w + kc][:, h * 64:(h + 1) * 64],
                       kc == 0, kc == 7, reads=[f"aTs{par}", f"vtm{4 * w + kc}"], writes=[pxn])
                vop(p, lambda e, h=h, px=px, par=par: e.tensor_scalar(s_otm[:, h * 64:(h + 1) * 64], px[:, 0:64], s_rs[par][:, 0:1], None, ALU.mult),
                    reads=[pxn, f"rsum{par}"], writes=["otm"])
            batched_T([(s_otm[:, r * 128:(r + 1) * 128], "otm") for r in range(4)],
                      s_ofm[:, (w * 4 + qt) * 512:(w * 4 + qt) * 512 + 512], "ofmb", qt % 2)
    rsrc = o_z if with_ln else f_in
    roff = 0 if with_ln else 513
    for w in range(4):
        c0 = w * 512
        for m in range(8):
            pp, ppn = P_C[m % 2], "PC0"
            for r in range(4):
                rv = s_ofm[:, :].rearrange("p (ch r t) -> p ch r t", r=4, t=128)[:, 4 * w:4 * w + 4, r, :]
                mm(p, pp[:], s_wo[r][:, m * 128:(m + 1) * 128], rv,
                   r == 0, r == 3, reads=[f"swo{r}", "ofmb"], writes=[ppn])
            fb, fbn = s_tm[m % 2], f"tm{m % 2}"
            dma(p, "sync", fb[:], rsrc[m * 128:(m + 1) * 128, roff + c0:roff + c0 + 512],
                reads=["zodram"] if with_ln else [], writes=[fbn], group=fbn)
            vop(p, lambda e, pp=pp, fb=fb: e.scalar_tensor_tensor(fb[:], pp[:], 1.0, fb[:], ALU.mult, ALU.add),
                reads=[ppn, fbn], writes=[fbn])
            dma(p, "gpsimd", o_f[m * 128:(m + 1) * 128, c0:c0 + 512], fb[:], reads=[fbn], group="fow")
    return p
def _rope_tables():
    n = SEQ
    dh = 64
    half = np.arange(0, dh, 2, dtype=np.float64)
    inv = 1.0 / 10000.0 ** (half / dh)
    t = np.arange(n, dtype=np.float64)
    fr = t[:, None] * inv[None, :]
    fr = np.concatenate([fr, fr], axis=-1)
    base = (half + 0.4 * dh) / (1.4 * dh)
    power = (t - n // 2) / 256.0
    sc = base[None, :] ** power[:, None]
    sc = np.concatenate([sc, sc], axis=-1)
    cos, sin = np.cos(fr), np.sin(fr)
    sgn = np.ones(dh)
    sgn[0:32] = -1.0
    cq = (8.0 * cos * sc).astype(np.float32)
    sq = (8.0 * sin * sc * sgn).astype(np.float32)
    ckk = (cos / sc).astype(np.float32)
    skk = (sin / sc * sgn).astype(np.float32)
    return cq, sq, ckk, skk


def _l1f_masks():
    NEG = np.float32(-3.3895314e38)
    qi = np.arange(512)
    out = np.zeros((2, 512, 1024), np.float32)
    for var in range(2):
        i = 512 + qi[:, None]
        j = np.arange(1024)[None, :]
        bad = (i < j) | (i > j + 512)
        if var == 1:
            bad = bad | (j < 512)
        out[var] = np.where(bad, NEG, 0.0)
    import ml_dtypes
    return out.astype(ml_dtypes.bfloat16)


def _prep_s6_maps(inputs, layer, f_full, with_ln):
    wqkv = np.ascontiguousarray(np.asarray(inputs["L1_ffn_wqkv"], np.float32)[layer])
    wov = np.ascontiguousarray(np.asarray(inputs["L1_ffn_wo"], np.float32)[layer])
    qs = np.asarray(inputs["L1_ffn_q_scale"], np.float32)[layer].reshape(1, 64)
    ks = np.asarray(inputs["L1_ffn_k_scale"], np.float32)[layer].reshape(1, 64)
    cq, sq, ckk, skk = _rope_tables()
    msk = _l1f_masks()
    cstv = np.zeros((128, 8), np.float32)
    cstv[:, 0] = 1e-5
    iden = np.eye(128, dtype=np.float32)
    fpad = np.pad(f_full, ((0, 0), (513, 0)))
    cqp = np.pad(cq, ((512, 0), (0, 0)))
    sqp = np.pad(sq, ((512, 0), (0, 0)))
    ckp = np.pad(ckk, ((512, 0), (0, 0)))
    skp = np.pad(skk, ((512, 0), (0, 0)))
    maps = []
    for c in range(NCORES):
        t0 = c * TC
        am = np.stack([msk[1 if (c == 0 and w == 0) else 0]
                       .reshape(4, 128, 1024).transpose(1, 0, 2).reshape(128, 4096)
                       for w in range(4)])
        am = np.ascontiguousarray(am)
        maps.append({
            "f": np.ascontiguousarray(fpad[:, t0:t0 + TH + 1]),
            "wqkv": wqkv, "wo": wov, "qsr": qs, "ksr": ks,
            "cq8": np.ascontiguousarray(cqp[t0:t0 + TH]),
            "snq": np.ascontiguousarray(sqp[t0:t0 + TH]),
            "ck": np.ascontiguousarray(ckp[t0:t0 + TH]),
            "snk": np.ascontiguousarray(skp[t0:t0 + TH]),
            "amask": am, "iden": iden, "cst": cstv,
        })
    return maps


def run_s6(inputs, layer, f_full, with_ln):
    p = build_s6(with_ln)
    nc = p.finish()
    maps = _prep_s6_maps(inputs, layer, f_full, with_ln)
    res = run_bass_kernel_spmd(nc, maps, core_ids=list(range(NCORES)))
    fo = np.concatenate([r["fo"] for r in res.results], axis=1)
    if with_ln:
        zo = np.concatenate([r["zo"] for r in res.results], axis=1)
        return fo, zo
    return fo, None


HID = 2730

def build_s7(final):
    p = Prog()
    f_in = p.dram("f", [D, TC + 1])
    w1 = p.dram("w1", [D, 2 * HID])
    w2 = p.dram("w2", [HID, D])
    prm = p.dram("prm", [128, 24])
    b1a = p.dram("b1a", [128, 22])
    b1g = p.dram("b1g", [128, 22])
    cst = p.dram("cst", [128, 8])
    if final:
        x3r = p.dram("x3r", [D, TC])
    o_f = p.dram("fo", [D, TC], kind="ExternalOutput")

    KT = [128] * 21 + [42]
    s_w1 = [p.sb(f"sw1_{k}", [128, 2 * HID], BF16) for k in range(8)]
    s_w2 = [p.sb(f"sw2_{k}", [128, D], BF16) for k in range(22)]
    for k in range(8):
        dma(p, "gpsimd", s_w1[k][:], w1[k * 128:(k + 1) * 128, :], writes=[f"sw1_{k}"], group="wl")
    for k in range(22):
        kw = KT[k]
        dma(p, "gpsimd", s_w2[k][0:kw, :], w2[k * 128:k * 128 + kw, :], writes=[f"sw2_{k}"], group="wl")
    p.fw.settle("wl", [f"sw1_{k}" for k in range(8)] + [f"sw2_{k}" for k in range(22)])
    s_prm = p.sb("c_prm", [128, 24])
    s_b1a = p.sb("c_b1a", [128, 22])
    s_b1g = p.sb("c_b1g", [128, 22])
    s_cst = p.sb("c_cst", [128, 8])
    s_ones1 = p.sb("ones1", [1, 128])
    s_onesb = p.sb("onesb", [128, 1], BF16)
    dma(p, "sync", s_prm[:], prm, writes=["prm"])
    dma(p, "sync", s_b1a[:], b1a, writes=["b1a"])
    dma(p, "sync", s_b1g[:], b1g, writes=["b1g"])
    dma(p, "sync", s_cst[:], cst, writes=["cst"])
    vop(p, lambda e: e.memset(s_ones1[:], 1.0), writes=["ones1"])
    vop(p, lambda e: e.memset(s_onesb[:], 1.0), writes=["onesb"])

    s_f = [p.sb(f"rf{i}", [128, 513]) for i in range(8)]
    s_fbf = [p.sb(f"fbf{i}", [128, 513], BF16) for i in range(8)]
    s_sqb = [p.sb(f"sqb{i}", [128, 513], BF16) for i in range(8)]
    s_xln = [p.sb(f"xlb{i}", [128, W], BF16) for i in range(8)]
    s_h2 = [p.sb(f"h2_{o}", [128, W], BF16) for o in range(22)]
    s_gl = p.sb("gl", [128, W], BF16)
    s_t1 = p.sb("t1s", [128, W])
    s_xr = p.sb("xrs", [128, W])
    s_mu = p.sb("mu", [1, W])
    s_var = p.sb("var", [1, W])
    s_rstd = p.sb("rstd", [1, W])
    P_sa = p.ps("Psa", [128, W])
    P_sb = p.ps("Psb", [128, W])
    P_mu = p.ps("Pmu", [128, W])
    P_rs = p.ps("Prs", [128, W])
    P_a = [p.ps("Pa0", [128, W]), p.ps("Pa1", [128, W])]
    P_g = [p.ps("Pg0", [128, W]), p.ps("Pg1", [128, W])]

    def shv(buf, i):
        return buf[:, 1:513] if i < 4 else buf[:, 0:512]

    for tt in range(NT):
        c0 = tt * W
        for i in range(8):
            dma(p, "sync", s_f[i][:], f_in[i * 128:(i + 1) * 128, c0:c0 + 513], writes=[f"rf{i}"])
            act(p, s_sqb[i][:], s_f[i][:], AF.Square, reads=[f"rf{i}"], writes=[f"sqb{i}"])
            vop(p, lambda e, i=i: e.tensor_copy(s_fbf[i][:], s_f[i][:]), reads=[f"rf{i}"], writes=[f"fbf{i}"])
        for i in range(8):
            mm(p, P_sa[0:1, 0:W], s_onesb[:], shv(s_fbf[i], i), i == 0, i == 7,
               reads=["onesb", f"fbf{i}"], writes=["Psa"])
        for i in range(8):
            mm(p, P_sb[0:1, 0:W], s_onesb[:], shv(s_sqb[i], i), i == 0, i == 7,
               reads=["onesb", f"sqb{i}"], writes=["Psb"])
        vop(p, lambda e: e.tensor_scalar_mul(s_mu[:], P_sa[0:1, 0:W], 1.0 / D), reads=["Psa"], writes=["mu"])
        vop(p, lambda e: e.tensor_scalar_mul(s_rstd[:], P_sb[0:1, 0:W], 1.0 / D), reads=["Psb"], writes=["rstd"])
        vop(p, lambda e: e.tensor_mul(s_var[:], s_mu[:], s_mu[:]), reads=["mu"], writes=["var"])
        vop(p, lambda e: e.tensor_sub(s_var[:], s_rstd[:], s_var[:]), reads=["rstd", "var"], writes=["var"])
        act(p, s_rstd[:], s_var[:], AF.Sqrt, bias=s_cst[0:1, 0:1], reads=["var", "cst"], writes=["rstd"])
        vop(p, lambda e: e.reciprocal(s_rstd[:], s_rstd[:]), reads=["rstd"], writes=["rstd"])
        mm(p, P_mu[0:128, 0:W], s_ones1[:], s_mu[0:1, 0:W], True, True, reads=["ones1", "mu"], writes=["Pmu"])
        mm(p, P_rs[0:128, 0:W], s_ones1[:], s_rstd[0:1, 0:W], True, True, reads=["ones1", "rstd"], writes=["Prs"])
        for i in range(8):
            vop(p, lambda e, i=i: e.scalar_tensor_tensor(s_t1[:], P_mu[:], -1.0, shv(s_f[i], i), ALU.mult, ALU.add),
                reads=["Pmu", f"rf{i}"], writes=["t1s"])
            vop(p, lambda e: e.tensor_mul(s_t1[:], s_t1[:], P_rs[:]), reads=["t1s", "Prs"], writes=["t1s"])
            vop(p, lambda e, i=i: e.tensor_scalar(s_xln[i][:], s_t1[:], s_prm[:, i:i + 1], s_prm[:, 8 + i:9 + i], ALU.mult, ALU.add),
                reads=["t1s", "prm"], writes=[f"xlb{i}"])
        for m in range(22):
            mw = KT[m]
            pa, pan = P_a[m % 2], f"Pa{m % 2}"
            pg, pgn = P_g[m % 2], f"Pg{m % 2}"
            for k in range(8):
                mm(p, pa[0:mw, :], s_w1[k][:, m * 128:m * 128 + mw], s_xln[k][:], k == 0, k == 7,
                   reads=[f"sw1_{k}", f"xlb{k}"], writes=[pan])
            for k in range(8):
                mm(p, pg[0:mw, :], s_w1[k][:, HID + m * 128:HID + m * 128 + mw], s_xln[k][:], k == 0, k == 7,
                   reads=[f"sw1_{k}", f"xlb{k}"], writes=[pgn])
            act(p, s_gl[0:mw, :], pg[0:mw, :], AF.Gelu, bias=s_b1g[0:mw, m:m + 1],
                reads=[pgn, "b1g"], writes=["gl"])
            vop(p, lambda e, m=m, mw=mw, pa=pa: e.scalar_tensor_tensor(s_h2[m][0:mw, :], pa[0:mw, :], s_b1a[0:mw, m:m + 1], s_gl[0:mw, :], ALU.add, ALU.mult),
                reads=[pan, "b1a", "gl"], writes=[f"h2_{m}"])
        for m in range(8):
            pa, pan = P_a[m % 2], f"Pa{m % 2}"
            for ko in range(22):
                mm(p, pa[:], s_w2[ko][0:KT[ko], m * 128:(m + 1) * 128], s_h2[ko][0:KT[ko], :],
                   ko == 0, ko == 21, reads=[f"sw2_{ko}", f"h2_{ko}"], writes=[pan])
            vop(p, lambda e, m=m, pa=pa: e.scalar_tensor_tensor(s_xr[:], pa[:], s_prm[:, 16 + m:17 + m], s_f[m][:, 1:513], ALU.add, ALU.add),
                reads=[pan, "prm", f"rf{m}"], writes=["xrs"])
            if final:
                dma(p, "sync", s_t1[:], x3r[m * 128:(m + 1) * 128, c0:c0 + W], writes=["t1s"])
                vop(p, lambda e: e.tensor_add(s_xr[:], s_xr[:], s_t1[:]), reads=["xrs", "t1s"], writes=["xrs"])
            dma(p, "gpsimd", o_f[m * 128:(m + 1) * 128, c0:c0 + W], s_xr[:], reads=["xrs"], group="fow")
    return p


def _prep_s7_maps(inputs, layer, f_full, final, x3_full=None):
    w1 = np.ascontiguousarray(np.asarray(inputs["L1_ffn_w1"], np.float32)[layer])
    w2 = np.ascontiguousarray(np.asarray(inputs["L1_ffn_w2"], np.float32)[layer])
    b1 = np.asarray(inputs["L1_ffn_b1"], np.float32)[layer]
    b1ac = np.zeros((128, 22), np.float32)
    b1gc = np.zeros((128, 22), np.float32)
    b1ap = np.pad(b1[:HID], (0, 22 * 128 - HID)).reshape(22, 128).T
    b1gp = np.pad(b1[HID:], (0, 22 * 128 - HID)).reshape(22, 128).T
    b1ac[:, :] = b1ap
    b1gc[:, :] = b1gp
    prm = np.zeros((128, 24), np.float32)
    prm[:, 0:8] = np.asarray(inputs["L1_ffn_ln_g"], np.float32)[layer].reshape(8, 128).T
    prm[:, 8:16] = np.asarray(inputs["L1_ffn_ln_b"], np.float32)[layer].reshape(8, 128).T
    prm[:, 16:24] = np.asarray(inputs["L1_ffn_b2"], np.float32)[layer].reshape(8, 128).T
    cstv = np.zeros((128, 8), np.float32)
    cstv[:, 0] = 1e-5
    fpad = np.pad(f_full, ((0, 0), (1, 0)))
    maps = []
    for c in range(NCORES):
        t0 = c * TC
        mp = {
            "f": np.ascontiguousarray(fpad[:, t0:t0 + TC + 1]),
            "w1": w1, "w2": w2, "prm": prm, "b1a": b1ac, "b1g": b1gc, "cst": cstv,
        }
        if final:
            mp["x3r"] = np.ascontiguousarray(x3_full[:, t0:t0 + TC])
        maps.append(mp)
    return maps


def run_s7(inputs, layer, f_full, final, x3_full=None):
    p = build_s7(final)
    nc = p.finish()
    maps = _prep_s7_maps(inputs, layer, f_full, final, x3_full)
    res = run_bass_kernel_spmd(nc, maps, core_ids=list(range(NCORES)))
    return np.concatenate([r["fo"] for r in res.results], axis=1)


def kernel(**inputs):
    xT = np.ascontiguousarray(np.asarray(inputs["L0_attn_x"], np.float32)[0].T)
    p = build_s1()
    nc = p.finish()
    res = run_bass_kernel_spmd(nc, _prep_s1_maps(inputs), core_ids=list(range(NCORES)))
    xln = np.concatenate([r["xln"] for r in res.results], axis=1)
    u = np.concatenate([r["u"] for r in res.results], axis=1)
    dt = np.concatenate([r["dt"] for r in res.results], axis=1)
    y = run_s2(inputs, dt, u)
    yf = np.zeros((D, SEQ), np.float32)
    for c in range(NCORES):
        yf[c * 128:(c + 1) * 128] = y[c * 128:(c + 1) * 128]
    x1 = run_s3(inputs, xT, xln, yf)
    x2 = run_s4(inputs, x1)
    x3 = run_s5(inputs, x2)
    f, _ = run_s6(inputs, 0, x3, True)
    f = run_s7(inputs, 0, f, False)
    f, _ = run_s6(inputs, 1, f, False)
    out = run_s7(inputs, 1, f, True, x3)
    return np.ascontiguousarray(out.T)[None].astype(np.float32)
